# version 14
# baseline (speedup 1.0000x reference)
import contextlib

import numpy as np
import ml_dtypes

import concourse.bass as bass
import concourse.mybir as mybir
from concourse.bass_utils import run_bass_kernel_spmd

F32 = mybir.dt.float32
BF16 = mybir.dt.bfloat16
AF = mybir.ActivationFunctionType
ALU = mybir.AluOpType
AX = mybir.AxisListType

LN_EPS = 1e-5

(LNS_SSUM, LNS_SSQ0, LNS_SSQ1, LNS_MU, LNS_MUSQ, LNS_VEPS, LNS_RINV, LNS_R0,
 LNS_R2, LNS_RSTD, LNS_D01, LNS_D10, LNS_P0, LNS_P1, LNS_V0, LNS_V1) = range(16)


class Cfg:
    def __init__(self, S=4096, D=1024, NH=16, HE=2048, NE=8, NGRP=4, dbg=False,
                 stop_after=None, dbg_taps=None):
        self.S, self.D, self.NH, self.HE, self.NE, self.NGRP, self.dbg = \
            S, D, NH, HE, NE, NGRP, dbg
        self.stop_after = stop_after
        self.dbg_taps = dbg_taps if dbg_taps is not None else {'p0', 'p1p', 'p1d', 'p3d'}
        self.HD = 64
        assert D == NH * self.HD
        self.T = S // 2
        self.DC = D // 128
        self.NKC = S // 128
        self.NTT = self.T // 128
        self.QB = min(512, self.T)
        self.NQB = self.T // self.QB
        self.KBLK = 512
        self.NKB = S // self.KBLK
        self.NKBo = self.T // self.KBLK
        self.HG = NH // NGRP
        self.PHG = self.HG // 2
        self.FH = self.HG * 64
        self.HC = HE // 128
        self.SL = min(512, self.T)
        self.NSL = self.T // self.SL
        self.G = 2
        self.NG = self.NKC // self.G
        assert D % 128 == 0 and self.T % 128 == 0 and S % 512 == 0
        assert self.HG % 2 == 0 and self.NKC % self.G == 0 and self.FH <= 512
        assert self.NTT <= 16


class KB:
    SEMS = ["sx", "sg", "sT", "sS", "sV", "v2", "s2"]

    def __init__(self, nc, st):
        self.nc = nc
        self.sem = {n: st.enter_context(nc.semaphore(n)) for n in self.SEMS}
        self.c = {n: 0 for n in self.SEMS}

    def inc(self, instr, n, k=1):
        instr.then_inc(self.sem[n], k)
        self.c[n] += k
        return self.c[n]

    def w(self, eng, n, v):
        if v > 0:
            eng.wait_ge(self.sem[n], v)

    def vsync(self, eng, ins, sem="v2"):
        n = self.inc(ins, sem)
        eng.wait_ge(self.sem[sem], n)


def _psum(nc, st, name, p, f, dt):
    per = 512 if dt == F32 else 1024
    fpad = ((f + per - 1) // per) * per
    t = st.enter_context(nc.psum_tensor(name, [p, fpad], dt))
    return t


def _ln_stats(v, kb, lns, ssq_c, x_ap, n_feat):
    ins = v.reduce_sum(lns[:, LNS_SSUM:LNS_SSUM + 1], x_ap, axis=AX.X)
    kb.vsync(v, ins)
    ins = v.tensor_scalar_mul(out=lns[:, LNS_MU:LNS_MU + 1],
                              in0=lns[:, LNS_SSUM:LNS_SSUM + 1], scalar1=1.0 / n_feat)
    kb.vsync(v, ins)
    ins = v.tensor_mul(lns[:, LNS_MUSQ:LNS_MUSQ + 1], lns[:, LNS_MU:LNS_MU + 1],
                       lns[:, LNS_MU:LNS_MU + 1])
    kb.vsync(v, ins)
    ins = v.scalar_tensor_tensor(out=lns[:, LNS_VEPS:LNS_VEPS + 1],
                                 in0=lns[:, ssq_c:ssq_c + 1], scalar=1.0 / n_feat,
                                 in1=lns[:, LNS_MUSQ:LNS_MUSQ + 1],
                                 op0=ALU.mult, op1=ALU.subtract)
    kb.vsync(v, ins)
    ins = v.tensor_scalar_add(out=lns[:, LNS_VEPS:LNS_VEPS + 1],
                              in0=lns[:, LNS_VEPS:LNS_VEPS + 1], scalar1=LN_EPS)
    kb.vsync(v, ins)
    return v.reciprocal(lns[:, LNS_RINV:LNS_RINV + 1], lns[:, LNS_VEPS:LNS_VEPS + 1])


def _ln_newton(v, kb, lns):
    ins = v.tensor_mul(lns[:, LNS_R2:LNS_R2 + 1], lns[:, LNS_R0:LNS_R0 + 1],
                       lns[:, LNS_R0:LNS_R0 + 1])
    kb.vsync(v, ins)
    ins = v.tensor_single_scalar(out=lns[:, LNS_R2:LNS_R2 + 1],
                                 in_=lns[:, LNS_R2:LNS_R2 + 1],
                                 scalar=lns[:, LNS_VEPS:LNS_VEPS + 1], op=ALU.mult)
    kb.vsync(v, ins)
    ins = v.tensor_scalar(out=lns[:, LNS_R2:LNS_R2 + 1], in0=lns[:, LNS_R2:LNS_R2 + 1],
                          scalar1=-0.5, scalar2=1.5, op0=ALU.mult, op1=ALU.add)
    kb.vsync(v, ins)
    ins = v.tensor_mul(lns[:, LNS_RSTD:LNS_RSTD + 1], lns[:, LNS_R0:LNS_R0 + 1],
                       lns[:, LNS_R2:LNS_R2 + 1])
    kb.vsync(v, ins)


RACE_DETECT = False


def build_program(C: Cfg):
    nc = bass.Bass("TRN2", target_bir_lowering=False, debug=False,
                   detect_race_conditions=RACE_DETECT)
    dt = nc.dram_tensor
    io = {}
    io["x_seq"] = dt("x_seq", [C.S, C.D], F32, kind="ExternalInput").ap()
    for n in ("wq", "wk", "wv", "wo"):
        io[n] = dt(n, [C.D, C.D], F32, kind="ExternalInput").ap()
    io["Wg"] = dt("Wg", [C.D, C.NE], F32, kind="ExternalInput").ap()
    io["W1"] = dt("W1", [C.NE, C.D, C.HE], F32, kind="ExternalInput").ap()
    io["W2"] = dt("W2", [C.NE, C.D, C.HE], F32, kind="ExternalInput").ap()
    io["W3"] = dt("W3", [C.NE, C.HE, C.D], F32, kind="ExternalInput").ap()
    io["identbf"] = dt("identbf", [128, 128], BF16, kind="ExternalInput").ap()
    io["identf32"] = dt("identf32", [128, 128], F32, kind="ExternalInput").ap()
    io["ones1"] = dt("ones1", [1, 128], BF16, kind="ExternalInput").ap()
    io["ebc"] = dt("ebc", [33, 128], BF16, kind="ExternalInput").ap()
    io["bqt"] = dt("bqt", [128, C.DC], F32, kind="ExternalInput").ap()
    io["bkt"] = dt("bkt", [128, C.DC], F32, kind="ExternalInput").ap()
    io["bot"] = dt("bot", [128, C.DC], F32, kind="ExternalInput").ap()
    io["gbbc"] = dt("gbbc", [128, 2 * C.D], F32, kind="ExternalInput").ap()
    io["bvbc"] = dt("bvbc", [128, C.D], F32, kind="ExternalInput").ap()
    io["attn_scr"] = dt("attn_scr", [C.T, C.D], BF16, kind="Internal").ap()
    io["out"] = dt("out", [C.T, C.D], F32, kind="ExternalOutput").ap()
    if C.dbg:
        io["dbg_xnfm"] = dt("dbg_xnfm", [128, C.DC * C.T], BF16, kind="ExternalOutput").ap()
        io["dbg_lns"] = dt("dbg_lns", [128, 16], F32, kind="ExternalOutput").ap()
        io["dbg_xntm"] = dt("dbg_xntm", [128, 2 * C.D], BF16, kind="ExternalOutput").ap()
        io["dbg_xin"] = dt("dbg_xin", [128, 2 * C.D], F32, kind="ExternalOutput").ap()
        io["dbg_wT8"] = dt("dbg_wT8", [8, C.T], F32, kind="ExternalOutput").ap()
        io["dbg_K"] = dt("dbg_K", [C.NGRP, 128, C.PHG * C.S], BF16, kind="ExternalOutput").ap()
        io["dbg_V"] = dt("dbg_V", [C.NGRP, 128, C.NKC * C.HG * 65], BF16, kind="ExternalOutput").ap()
        io["dbg_Q"] = dt("dbg_Q", [C.NGRP, 128, C.PHG * C.T], BF16, kind="ExternalOutput").ap()
        io["dbg_ctx"] = dt("dbg_ctx", [128, (C.NH // 2) * C.T], BF16, kind="ExternalOutput").ap()
        io["dbg_acc"] = dt("dbg_acc", [128, C.DC * C.T], F32, kind="ExternalOutput").ap()

    with contextlib.ExitStack() as st:
        kb = KB(nc, st)
        E = st.enter_context
        sb = {}
        sb["xn_fm"] = E(nc.sbuf_tensor("xn_fm", [128, C.DC * C.T], BF16))
        sb["wT8"] = E(nc.sbuf_tensor("wT8", [8, C.T], F32))
        sb["idb"] = E(nc.sbuf_tensor("idb", [128, 128], BF16))
        sb["idf"] = E(nc.sbuf_tensor("idf", [128, 128], F32))
        sb["ones"] = E(nc.sbuf_tensor("ones", [1, 128], BF16))
        sb["ebc"] = E(nc.sbuf_tensor("ebc_s", [33, 128], BF16))
        sb["bqt"] = E(nc.sbuf_tensor("bqt_s", [128, C.DC], F32))
        sb["bkt"] = E(nc.sbuf_tensor("bkt_s", [128, C.DC], F32))
        sb["bot"] = E(nc.sbuf_tensor("bot_s", [128, C.DC], F32))
        sb["Wg"] = E(nc.sbuf_tensor("Wgs", [128, C.DC * C.NE], BF16))
        sb["lns"] = E(nc.sbuf_tensor("lns", [128, 16], F32))
        sb["w_tm"] = E(nc.sbuf_tensor("w_tm", [128, C.NTT * C.NE], F32))
        sb["gsc"] = E(nc.sbuf_tensor("gsc", [128, 4 * C.NE], F32))
        sb["z8"] = E(nc.sbuf_tensor("z8", [128, 8], F32))
        sb["n8"] = E(nc.sbuf_tensor("n8", [128, 8], F32))

        with contextlib.ExitStack() as p02:
            sb["xn_oth"] = p02.enter_context(
                nc.sbuf_tensor("xn_oth", [128, C.DC * C.T], BF16))
            _p0(nc, kb, C, io, sb, st)

            with contextlib.ExitStack() as p1s:
                sb["ctx"] = p1s.enter_context(
                    nc.sbuf_tensor("ctx", [128, (C.NH // 2) * C.T], BF16))
                with contextlib.ExitStack() as p1o:
                    E1 = p1o.enter_context
                    sb["wq_g"] = E1(nc.sbuf_tensor("wq_g", [128, C.DC * C.FH], BF16))
                    sb["wk_g"] = E1(nc.sbuf_tensor("wk_g", [128, C.DC * C.FH], BF16))
                    sb["wv_g"] = E1(nc.sbuf_tensor("wv_g", [128, C.DC * C.FH], BF16))
                    sb["K"] = E1(nc.sbuf_tensor("Ksb", [128, C.PHG * C.S], BF16))
                    sb["V"] = E1(nc.sbuf_tensor("Vsb", [128, C.NKC * C.HG * 65], BF16))
                    sb["Q"] = E1(nc.sbuf_tensor("Qsb", [128, C.PHG * C.T], BF16))
                    sb["p_sb"] = E1(nc.sbuf_tensor("p_sb", [128, 3 * C.G * C.QB], BF16))
                    sb["rZ2"] = E1(nc.sbuf_tensor("rZ2", [33, C.QB], BF16))
                    sb["bc_sb"] = E1(nc.sbuf_tensor("bc_sb", [128, C.QB], BF16))
                    sb["bvbc"] = E1(nc.sbuf_tensor("bvbcs", [128, C.D], F32))
                    for grp in range(C.NGRP if C.stop_after != "p0" else 0):
                        _p1_proj(nc, kb, C, grp, io, sb)
                        if C.stop_after == "p1proj" and grp == 0:
                            break
                        _p1_attn(nc, kb, C, grp, io, sb)
                    if C.stop_after in ("p0", "p1proj", "p1"):
                        return nc
                    if C.dbg and 'p1d' in C.dbg_taps:
                        with nc.Block("p1d") as blk:
                            @blk.gpsimd
                            def _(g):
                                kb.inc(g.dma_start(out=io["dbg_ctx"], in_=sb["ctx"][:, :]), "sg", 16)
                                g.wait_ge(kb.sem["sg"], kb.c["sg"])
                _p2(nc, kb, C, io, sb)
        with contextlib.ExitStack() as p3s:
            E3 = p3s.enter_context
            sb["acc"] = E3(nc.sbuf_tensor("acc", [128, C.DC * C.T], F32))
            sb["W1s"] = E3(nc.sbuf_tensor("W1s", [128, C.DC * C.HE], BF16))
            sb["W2s"] = E3(nc.sbuf_tensor("W2s", [128, C.DC * C.HE], BF16))
            _p3(nc, kb, C, io, sb, p3s)
            if C.dbg and 'p3d' in C.dbg_taps:
                with nc.Block("p3d") as blk:
                    @blk.gpsimd
                    def _(g):
                        kb.inc(g.dma_start(out=io["dbg_acc"], in_=sb["acc"][:, :]), "sg", 16)
                        g.wait_ge(kb.sem["sg"], kb.c["sg"])
            _p4(nc, kb, C, io, sb)
    return nc


def _p0(nc, kb, C, io, sb, st_outer):
    b = dict(kb.c)
    NT = C.NTT
    NT2 = 2 * NT
    NCONST = 7
    rec = {"tp": [None] * NT2, "lg": None, "lgt": [None] * NT, "wt": [None] * NT}
    xn_fm, lns, gsc, w_tm, wT8 = sb["xn_fm"], sb["lns"], sb["gsc"], sb["w_tm"], sb["wT8"]
    BV = 3 * NT + 2 + 2 * NT + 1

    def v_rinv(i):
        return b["sV"] + (3 * i + 1 if i < NT else BV + 3 * (i - NT) + 1)

    def v_xntm(i):
        return v_rinv(i) + 1

    def v_evac(i):
        return v_rinv(i) + 2

    def s_sq(i):
        return b["sS"] + (2 * i + 1 if i < NT else 3 * NT + 2 * (i - NT) + 1)

    def s_r0(i):
        return s_sq(i) + 1

    with contextlib.ExitStack() as ps:
        E0 = ps.enter_context
        xin = E0(nc.sbuf_tensor("xin", [128, 2 * C.D], F32))
        xntm = E0(nc.sbuf_tensor("xntm", [128, 2 * C.D], BF16))
        sqd = E0(nc.sbuf_tensor("sqd", [128, C.D], BF16))
        lgf = E0(nc.sbuf_tensor("lgf", [8, C.T], F32))
        lgtm = E0(nc.sbuf_tensor("lgtm", [128, C.NTT * C.NE], F32))
        pst = [_psum(nc, ps, f"pst{i}", 128, C.D, BF16) for i in range(2)]
        ps_g = _psum(nc, ps, "ps_g", 8, C.T, F32)
        ps_t = _psum(nc, ps, "ps_t", 128, NT * C.NE, F32)
        blk = E0(nc.Block("p0"))

        def ln_dst(i):
            if i < NT:
                return bass.AP(xn_fm, i * 128, [[C.DC * C.T, 128], [C.T, C.DC], [1, 128]])
            return bass.AP(sb["xn_oth"], (i - NT) * 128,
                           [[C.DC * C.T, 128], [C.T, C.DC], [1, 128]])

        @blk.gpsimd
        def _(g):
            kb.inc(g.dma_start(
                out=bass.AP(sb["Wg"], 0, [[C.DC * C.NE, 128], [C.NE, C.DC], [1, C.NE]]),
                in_=io["Wg"].rearrange("(c p) e -> p c e", p=128)), "sg", 16)

        @blk.sync
        def _(s):
            for name in ("identbf", "identf32", "ones1", "ebc", "bqt", "bkt", "bot"):
                tgt = {"identbf": "idb", "identf32": "idf", "ones1": "ones",
                       "ebc": "ebc", "bqt": "bqt", "bkt": "bkt", "bot": "bot"}[name]
                kb.inc(s.dma_start(out=sb[tgt][:, :], in_=io[name]), "sx", 16)
            for i in range(NT2):
                if i >= 2:
                    kb.w(s, "sV", v_xntm(i - 2))
                kb.inc(s.dma_start(out=xin[:, (i % 2) * C.D:(i % 2 + 1) * C.D],
                                   in_=io["x_seq"][i * 128:(i + 1) * 128, :]), "sx", 16)

        def emit_ln_tensor(t, i):
            sl = i % 2
            kb.w(t, "sV", v_xntm(i))
            if i >= 2:
                kb.w(t, "sV", v_evac(i - 2))
            for dc in range(C.DC):
                ins = t.transpose(pst[sl][:, dc * 128:(dc + 1) * 128],
                                  xntm[:, sl * C.D + dc * 128:sl * C.D + (dc + 1) * 128],
                                  sb["idb"][:, :])
            rec["tp"][i] = kb.inc(ins, "sT")

        @blk.tensor
        def _(t):
            for i in range(NT):
                emit_ln_tensor(t, i)
            kb.w(t, "sg", b["sg"] + 16)
            kb.w(t, "sV", b["sV"] + 3 * NT)
            for qb in range(C.T // 512):
                for dc in range(C.DC):
                    ins = t.matmul(ps_g[0:C.NE, qb * 512:(qb + 1) * 512],
                                   bass.AP(sb["Wg"], dc * C.NE, [[C.DC * C.NE, 128], [1, C.NE]]),
                                   xn_fm[:, dc * C.T + qb * 512:dc * C.T + (qb + 1) * 512],
                                   start=(dc == 0), stop=(dc == C.DC - 1))
            rec["lg"] = kb.inc(ins, "sT")
            kb.w(t, "sV", b["sV"] + 3 * NT + 1)
            for i in range(NT):
                ins = t.transpose(ps_t[:, i * C.NE:i * C.NE + C.NE],
                                  lgf[0:8, i * 128:(i + 1) * 128], sb["idf"][0:8, 0:8])
                rec["lgt"][i] = kb.inc(ins, "sT")
            for i in range(NT):
                kb.w(t, "sV", b["sV"] + 3 * NT + 2 + 2 * i + 2)
                ins = t.transpose(ps_g[0:C.NE, i * 128:(i + 1) * 128],
                                  w_tm[:, i * C.NE:(i + 1) * C.NE], sb["idf"][:, :])
                rec["wt"][i] = kb.inc(ins, "sT")
            for i in range(NT, NT2):
                emit_ln_tensor(t, i)

        def emit_ln_scalar(s, i):
            sl = i % 2
            kb.w(s, "sx", b["sx"] + 16 * (NCONST + i + 1))
            if i >= 2:
                kb.w(s, "sV", v_rinv(i - 2))
            ins = s.activation(out=sqd[:, :], in_=xin[:, sl * C.D:(sl + 1) * C.D],
                               func=AF.Square,
                               accum_out=lns[:, LNS_SSQ0 + sl:LNS_SSQ0 + sl + 1])
            kb.inc(ins, "sS")
            kb.w(s, "sV", v_rinv(i))
            ins = s.activation(out=lns[:, LNS_R0:LNS_R0 + 1],
                               in_=lns[:, LNS_RINV:LNS_RINV + 1], func=AF.Sqrt)
            kb.inc(ins, "sS")

        @blk.scalar
        def _(s):
            for i in range(NT):
                emit_ln_scalar(s, i)
            for i in range(NT):
                kb.w(s, "sV", b["sV"] + 3 * NT + 2 + 2 * i + 1)
                s.activation(out=lns[:, LNS_P0:LNS_P0 + 1],
                             in_=lns[:, LNS_D01:LNS_D01 + 1], func=AF.Sigmoid)
                ins = s.activation(out=lns[:, LNS_P1:LNS_P1 + 1],
                                   in_=lns[:, LNS_D10:LNS_D10 + 1], func=AF.Sigmoid)
                kb.inc(ins, "sS")
            for i in range(NT, NT2):
                emit_ln_scalar(s, i)

        def emit_ln_vector(v, i):
            sl = i % 2
            kb.w(v, "sx", b["sx"] + 16 * (NCONST + i + 1))
            kb.w(v, "sS", s_sq(i))
            ins = _ln_stats(v, kb, lns, LNS_SSQ0 + sl, xin[:, sl * C.D:(sl + 1) * C.D], C.D)
            kb.inc(ins, "sV")
            kb.w(v, "sS", s_r0(i))
            _ln_newton(v, kb, lns)
            if i >= 2:
                kb.w(v, "sT", rec["tp"][i - 2])
            ins = v.tensor_scalar(out=xntm[:, sl * C.D:(sl + 1) * C.D],
                                  in0=xin[:, sl * C.D:(sl + 1) * C.D],
                                  scalar1=lns[:, LNS_MU:LNS_MU + 1],
                                  scalar2=lns[:, LNS_RSTD:LNS_RSTD + 1],
                                  op0=ALU.subtract, op1=ALU.mult)
            kb.inc(ins, "sV")
            kb.w(v, "sT", rec["tp"][i])
            ins = v.tensor_copy(out=ln_dst(i), in_=pst[sl][:, 0:C.D])
            kb.inc(ins, "sV")

        @blk.vector
        def _(v):
            for i in range(NT):
                emit_ln_vector(v, i)
            kb.w(v, "sT", rec["lg"])
            ins = v.tensor_copy(out=lgf[0:8, :], in_=ps_g[0:8, 0:C.T])
            kb.inc(ins, "sV")
            kb.w(v, "sT", rec["lgt"][NT - 1])
            ins = v.tensor_copy(out=lgtm[:, :], in_=ps_t[:, 0:NT * C.NE])
            kb.vsync(v, ins)
            ins = v.memset(sb["z8"][:, :], 0.0)
            kb.inc(ins, "sV")
            ins = v.memset(sb["z8"][:, :], 0.0)
            kb.vsync(v, ins)
            for i in range(NT):
                lg = lgtm[:, i * C.NE:(i + 1) * C.NE]
                ins = v.reduce_max(lns[:, LNS_V0:LNS_V0 + 1], lg, axis=AX.X)
                kb.vsync(v, ins)
                ins = v.tensor_scalar_add(out=sb["n8"][:, :], in0=sb["z8"][:, :],
                                          scalar1=lns[:, LNS_V0:LNS_V0 + 1])
                kb.vsync(v, ins)
                ins = v.match_replace(out=gsc[:, 0:C.NE], in_to_replace=sb["n8"][:, 0:8],
                                      in_values=lg, imm_value=-1e30)
                kb.vsync(v, ins)
                ins = v.reduce_max(lns[:, LNS_V1:LNS_V1 + 1], gsc[:, 0:C.NE], axis=AX.X)
                kb.vsync(v, ins)
                ins = v.tensor_scalar_add(out=sb["n8"][:, :], in0=sb["z8"][:, :],
                                          scalar1=lns[:, LNS_V1:LNS_V1 + 1])
                kb.vsync(v, ins)
                ins = v.match_replace(out=gsc[:, C.NE:2 * C.NE], in_to_replace=sb["n8"][:, 0:8],
                                      in_values=gsc[:, 0:C.NE], imm_value=-1e30)
                kb.vsync(v, ins)
                v.tensor_tensor(out=gsc[:, 2 * C.NE:3 * C.NE], in0=lg,
                                in1=gsc[:, 0:C.NE], op=ALU.not_equal)
                ins = v.tensor_tensor(out=gsc[:, 3 * C.NE:4 * C.NE], in0=gsc[:, 0:C.NE],
                                      in1=gsc[:, C.NE:2 * C.NE], op=ALU.not_equal)
                kb.vsync(v, ins)
                ins = v.tensor_sub(lns[:, LNS_D01:LNS_D01 + 1], lns[:, LNS_V0:LNS_V0 + 1],
                                   lns[:, LNS_V1:LNS_V1 + 1])
                kb.vsync(v, ins)
                ins = v.tensor_scalar_mul(out=lns[:, LNS_D10:LNS_D10 + 1],
                                          in0=lns[:, LNS_D01:LNS_D01 + 1], scalar1=-1.0)
                kb.inc(ins, "sV")
                kb.w(v, "sS", b["sS"] + 2 * NT + i + 1)
                ins = v.tensor_single_scalar(out=gsc[:, 2 * C.NE:3 * C.NE],
                                             in_=gsc[:, 2 * C.NE:3 * C.NE],
                                             scalar=lns[:, LNS_P0:LNS_P0 + 1], op=ALU.mult)
                kb.vsync(v, ins)
                ins = v.scalar_tensor_tensor(out=w_tm[:, i * C.NE:(i + 1) * C.NE],
                                             in0=gsc[:, 3 * C.NE:4 * C.NE],
                                             scalar=lns[:, LNS_P1:LNS_P1 + 1],
                                             in1=gsc[:, 2 * C.NE:3 * C.NE],
                                             op0=ALU.mult, op1=ALU.add)
                kb.inc(ins, "sV")
            kb.w(v, "sT", rec["wt"][NT - 1])
            ins = v.tensor_copy(out=wT8[0:8, :], in_=ps_g[0:8, 0:C.T])
            kb.inc(ins, "sV")
            for i in range(NT, NT2):
                emit_ln_vector(v, i)

        if C.dbg and 'p0' in C.dbg_taps:
            @blk.gpsimd
            def _(g):
                g.wait_ge(kb.sem["sV"], kb.c["sV"])
                g.wait_ge(kb.sem["sS"], kb.c["sS"])
                kb.inc(g.dma_start(out=io["dbg_xnfm"], in_=xn_fm[:, :]), "sg", 16)
                kb.inc(g.dma_start(out=io["dbg_wT8"], in_=wT8[0:8, :]), "sg", 16)
                kb.inc(g.dma_start(out=io["dbg_lns"], in_=lns[:, :]), "sg", 16)
                kb.inc(g.dma_start(out=io["dbg_xntm"], in_=xntm[:, :]), "sg", 16)
                kb.inc(g.dma_start(out=io["dbg_xin"], in_=xin[:, :]), "sg", 16)
                g.wait_ge(kb.sem["sg"], kb.c["sg"])


def _p1_proj(nc, kb, C, grp, io, sb):
    b = dict(kb.c)
    NQ = C.PHG * C.NQB
    NV = C.NKB * 4
    NKp = C.NKB * C.PHG
    VW = C.HG * 65
    xn_fm, xn_oth = sb["xn_fm"], sb["xn_oth"]
    K_sb, V_sb, Q_sb = sb["K"], sb["V"], sb["Q"]

    def xsrc(dc, t0, nn):
        if t0 < C.T:
            return xn_fm[:, dc * C.T + t0:dc * C.T + t0 + nn]
        return xn_oth[:, dc * C.T + t0 - C.T:dc * C.T + t0 - C.T + nn]

    rec = {"qps": [None] * NQ, "kps": [None] * NKp, "vps": [None] * NV}

    def s_ev(i):
        return b["sS"] + i + 1

    with contextlib.ExitStack() as ps:
        pa = [_psum(nc, ps, f"pa{grp}_{i}", 128, max(C.QB, C.KBLK), F32) for i in range(2)]
        pv = [_psum(nc, ps, f"pvv{grp}_{i}", 128, C.FH, F32) for i in range(2)]
        blk = ps.enter_context(nc.Block(f"p1p{grp}"))

        @blk.gpsimd
        def _(g):
            for wn, wsb in (("wq", "wq_g"), ("wk", "wk_g"), ("wv", "wv_g")):
                src = io[wn][:, grp * C.FH:(grp + 1) * C.FH].rearrange("(c p) f -> p c f", p=128)
                kb.inc(g.dma_start(
                    out=bass.AP(sb[wsb], 0, [[C.DC * C.FH, 128], [C.FH, C.DC], [1, C.FH]]),
                    in_=src), "sg", 16)
            if grp == 0:
                kb.inc(g.dma_start(out=sb["bvbc"][:, :], in_=io["bvbc"]), "sg", 16)

        @blk.tensor
        def _(t):
            kb.w(t, "sg", b["sg"] + 16)
            a_idx = 0
            for pp in range(C.PHG):
                for qb in range(C.NQB):
                    sl = pa[a_idx % 2]
                    if a_idx >= 2:
                        kb.w(t, "sS", s_ev(a_idx - 2))
                    for dc in range(C.DC):
                        ins = t.matmul(sl[:, 0:C.QB],
                                       sb["wq_g"][:, dc * C.FH + pp * 128:dc * C.FH + (pp + 1) * 128],
                                       xn_fm[:, dc * C.T + qb * C.QB:dc * C.T + (qb + 1) * C.QB],
                                       start=(dc == 0), stop=(dc == C.DC - 1))
                    rec["qps"][a_idx] = kb.inc(ins, "sT")
                    a_idx += 1
            kb.w(t, "sg", b["sg"] + 48)
            k_idx = 0
            v_idx = 0
            for j in range(C.NKB):
                for pp in range(C.PHG):
                    sl = pa[a_idx % 2]
                    if a_idx >= 2:
                        kb.w(t, "sS", s_ev(a_idx - 2))
                    for dc in range(C.DC):
                        ins = t.matmul(sl[:, 0:C.KBLK],
                                       sb["wk_g"][:, dc * C.FH + pp * 128:dc * C.FH + (pp + 1) * 128],
                                       xsrc(dc, j * C.KBLK, C.KBLK),
                                       start=(dc == 0), stop=(dc == C.DC - 1))
                    rec["kps"][k_idx] = kb.inc(ins, "sT")
                    a_idx += 1
                    k_idx += 1
                for tt in range(4):
                    sl = pv[v_idx % 2]
                    if v_idx >= 2:
                        kb.w(t, "sV", b["sV"] + (v_idx - 2) + 1)
                    for dc in range(C.DC):
                        ins = t.matmul(sl[:, 0:C.FH],
                                       xsrc(dc, j * C.KBLK + tt * 128, 128),
                                       sb["wv_g"][:, dc * C.FH:(dc + 1) * C.FH],
                                       start=(dc == 0), stop=(dc == C.DC - 1))
                    rec["vps"][v_idx] = kb.inc(ins, "sT")
                    v_idx += 1

        @blk.scalar
        def _(s):
            for i in range(NQ):
                pp, qb = divmod(i, C.NQB)
                pg = grp * C.PHG + pp
                kb.w(s, "sT", rec["qps"][i])
                ins = s.activation(
                    out=Q_sb[:, pp * C.T + qb * C.QB:pp * C.T + (qb + 1) * C.QB],
                    in_=pa[i % 2][:, 0:C.QB], func=AF.Identity,
                    bias=sb["bqt"][:, pg:pg + 1])
                kb.inc(ins, "sS")
            for kk in range(NKp):
                bi, pp = divmod(kk, C.PHG)
                pg = grp * C.PHG + pp
                kb.w(s, "sT", rec["kps"][kk])
                ins = s.activation(
                    out=K_sb[:, pp * C.S + bi * C.KBLK:pp * C.S + (bi + 1) * C.KBLK],
                    in_=pa[(NQ + kk) % 2][:, 0:C.KBLK], func=AF.Identity,
                    bias=sb["bkt"][:, pg:pg + 1])
                kb.inc(ins, "sS")

        @blk.vector
        def _(v):
            if grp == 0:
                v.memset(bass.AP(V_sb, 64, [[C.NKC * VW, 128], [65, C.NKC * C.HG], [1, 1]]), 1.0)
                v.memset(sb["rZ2"][:, :], 0.0)
            for v_idx in range(NV):
                kb.w(v, "sT", rec["vps"][v_idx])
                if grp == 0:
                    kb.w(v, "sg", b["sg"] + 64)
                kt = v_idx
                ins = v.scalar_tensor_tensor(
                    out=bass.AP(V_sb, kt * VW, [[C.NKC * VW, 128], [65, C.HG], [1, 64]]),
                    in0=pv[v_idx % 2][:, 0:C.FH], scalar=1.0,
                    in1=sb["bvbc"][:, grp * C.FH:(grp + 1) * C.FH],
                    op0=ALU.mult, op1=ALU.add)
                kb.inc(ins, "sV")

    if C.dbg and 'p1p' in C.dbg_taps:
        with nc.Block(f"p1pd{grp}") as blk:
            @blk.gpsimd
            def _(g):
                kb.inc(g.dma_start(out=io["dbg_K"][grp], in_=K_sb[:, :]), "sg", 16)
                kb.inc(g.dma_start(out=io["dbg_V"][grp], in_=V_sb[:, :]), "sg", 16)
                kb.inc(g.dma_start(out=io["dbg_Q"][grp], in_=Q_sb[:, :]), "sg", 16)
                g.wait_ge(kb.sem["sg"], kb.c["sg"])

def _p1_attn(nc, kb, C, grp, io, sb):
    b = dict(kb.c)
    VW = C.HG * 65
    K_sb, V_sb, Q_sb = sb["K"], sb["V"], sb["Q"]
    p_sb, rZ2, ctx = sb["p_sb"], sb["rZ2"], sb["ctx"]
    NPQ = C.PHG * C.NQB
    NGG = NPQ * 2 * C.NG
    rec = {"s": [None] * NGG, "av": [None] * NGG, "bc": [None] * NPQ}

    with contextlib.ExitStack() as ps:
        ps_s = [_psum(nc, ps, f"pss{grp}_{i}", 128, C.G * C.QB, F32) for i in range(2)]
        ps_ctx = [_psum(nc, ps, f"psc{grp}_{i}", 128, C.QB, F32) for i in range(2)]
        ps_bc = _psum(nc, ps, f"psbc{grp}", 128, C.QB, F32)
        blk = ps.enter_context(nc.Block(f"p1a{grp}"))

        its = []
        for p in range(C.PHG):
            for qb in range(C.NQB):
                for kc in range(C.NKC):
                    its.append((p, qb, p * C.NQB + qb, kc))
        NIT = len(its)
        rec["s"] = [None] * NIT
        rec["av"] = [None] * NIT

        @blk.tensor
        def _(t):
            def emit_S2(it):
                p, qb, i_pq, kc = its[it]
                sl = it % 2
                if it >= 2:
                    kb.w(t, "sS", b["sS"] + it - 1)
                for hl in range(2):
                    ins = t.matmul(
                        ps_s[sl][:, hl * C.QB:(hl + 1) * C.QB],
                        K_sb[hl * 64:hl * 64 + 64,
                             p * C.S + kc * 128:p * C.S + (kc + 1) * 128],
                        Q_sb[hl * 64:hl * 64 + 64,
                             p * C.T + qb * C.QB:p * C.T + (qb + 1) * C.QB],
                        start=True, stop=True)
                rec["s"][it] = kb.inc(ins, "sT")

            def emit_AV2(it):
                p, qb, i_pq, kc = its[it]
                sl = it % 2
                kb.w(t, "sS", b["sS"] + it + 1)
                if kc == 0:
                    kb.w(t, "sV", b["sV"] + 2 * i_pq)
                for hl in range(2):
                    lh = p * 2 + hl
                    ins = t.matmul(
                        ps_ctx[hl][0:65, :],
                        V_sb[:, kc * VW + lh * 65:kc * VW + (lh + 1) * 65],
                        p_sb[:, ((it % 3) * 2 + hl) * C.QB:((it % 3) * 2 + hl + 1) * C.QB],
                        start=(kc == 0), stop=(kc == C.NKC - 1))
                rec["av"][it] = kb.inc(ins, "sT")

            emit_S2(0)
            for it in range(NIT):
                if it + 1 < NIT:
                    emit_S2(it + 1)
                emit_AV2(it)
                p, qb, i_pq, kc = its[it]
                if kc == C.NKC - 1:
                    kb.w(t, "sV", b["sV"] + 2 * i_pq + 1)
                    ins = t.matmul(ps_bc[:, 0:C.QB], sb["ebc"][0:33, :],
                                   rZ2[0:33, 0:C.QB], start=True, stop=True)
                    rec["bc"][i_pq] = kb.inc(ins, "sT")

        @blk.scalar
        def _(s):
            for it in range(NIT):
                sl = it % 2
                kb.w(s, "sT", rec["s"][it])
                if it >= 3:
                    kb.w(s, "sT", rec["av"][it - 3])
                ins = s.activation(out=p_sb[:, (it % 3) * 2 * C.QB:(it % 3 + 1) * 2 * C.QB],
                                   in_=ps_s[sl][:, 0:2 * C.QB], func=AF.Exp, scale=0.125)
                kb.inc(ins, "sS")

        @blk.vector
        def _(v):
            for p in range(C.PHG):
                for qb in range(C.NQB):
                    i_pq = p * C.NQB + qb
                    GG_last = (i_pq + 1) * C.NKC - 1
                    kb.w(v, "sT", rec["av"][GG_last])
                    with nc.allow_low_precision(reason="1/Z feeds a bf16 broadcast matmul"):
                        v.reciprocal(rZ2[0:1, 0:C.QB], ps_ctx[0][64:65, 0:C.QB])
                        ins = v.reciprocal(rZ2[32:33, 0:C.QB], ps_ctx[1][64:65, 0:C.QB])
                    kb.inc(ins, "sV")
                    kb.w(v, "sT", rec["bc"][i_pq])
                    pg = grp * C.PHG + p
                    with nc.allow_low_precision(reason="1/Z bcast is a bf16 scale factor"):
                        ins = v.tensor_copy(out=sb["bc_sb"][:, 0:C.QB], in_=ps_bc[:, 0:C.QB])
                    kb.vsync(v, ins)
                    v.tensor_mul(ctx[0:64, pg * C.T + qb * C.QB:pg * C.T + (qb + 1) * C.QB],
                                 ps_ctx[0][0:64, 0:C.QB], sb["bc_sb"][0:64, 0:C.QB])
                    ins = v.tensor_mul(
                        ctx[64:128, pg * C.T + qb * C.QB:pg * C.T + (qb + 1) * C.QB],
                        ps_ctx[1][0:64, 0:C.QB], sb["bc_sb"][64:128, 0:C.QB])
                    kb.inc(ins, "sV")


def _p2(nc, kb, C, io, sb):
    b = dict(kb.c)
    NT4 = C.QB // 128
    NTI = C.NQB * NT4

    def v_z(zi):
        qb, oc = divmod(zi, C.DC)
        return b["sV"] + qb * (C.DC + NT4 * 3) + oc + 1

    def v_zall(qb):
        return b["sV"] + qb * (C.DC + NT4 * 3) + C.DC

    def v_ztm(ti):
        qb, tt = divmod(ti, NT4)
        return b["sV"] + qb * (C.DC + NT4 * 3) + C.DC + tt * 3 + 1

    def v_rinv(ti):
        return v_ztm(ti) + 1

    def v_ast(ti):
        return v_ztm(ti) + 2

    rec = {"o": [None] * (C.NQB * C.DC), "tp": [None] * NTI}
    xn_fm, lns, ctx = sb["xn_fm"], sb["lns"], sb["ctx"]

    with contextlib.ExitStack() as ps:
        E2 = ps.enter_context
        wo_sb = E2(nc.sbuf_tensor("wo_sb", [128, C.DC * C.D], BF16))
        gb_sb = E2(nc.sbuf_tensor("gb_sb", [128, 2 * C.D], F32))
        z_fm = E2(nc.sbuf_tensor("z_fm", [128, 2 * C.DC * C.QB], F32))
        z_tm = E2(nc.sbuf_tensor("z_tm", [128, 2 * C.D], F32))
        t1 = E2(nc.sbuf_tensor("t1", [128, C.D], F32))
        sqd2 = E2(nc.sbuf_tensor("sqd2", [128, C.D], BF16))
        ast = E2(nc.sbuf_tensor("ast", [128, 2 * C.D], BF16))
        po = [_psum(nc, ps, f"po{i}", 128, C.QB, F32) for i in range(2)]
        pz = [_psum(nc, ps, f"pz{i}", 128, C.D, F32) for i in range(2)]
        blk = E2(nc.Block("p2"))

        @blk.gpsimd
        def _(g):
            kb.inc(g.dma_start(
                out=bass.AP(wo_sb, 0, [[C.DC * C.D, 128], [C.D, C.DC], [1, C.D]]),
                in_=io["wo"].rearrange("(c p) f -> p c f", p=128)), "sg", 16)

        @blk.sync
        def _(s):
            kb.inc(s.dma_start(out=gb_sb[:, :], in_=io["gbbc"]), "sx", 16)
            for ti in range(NTI):
                kb.w(s, "sV", v_ast(ti))
                kb.inc(s.dma_start(out=io["attn_scr"][ti * 128:(ti + 1) * 128, :],
                                   in_=ast[:, (ti % 2) * C.D:(ti % 2 + 1) * C.D]), "sx", 16)

        @blk.tensor
        def _(t):
            kb.w(t, "sg", b["sg"] + 16)
            for qb in range(C.NQB):
                for oc in range(C.DC):
                    zi = qb * C.DC + oc
                    if zi >= 2:
                        kb.w(t, "sV", v_z(zi - 2))
                    for cc in range(C.DC):
                        ins = t.matmul(po[zi % 2][:, 0:C.QB],
                                       wo_sb[:, cc * C.D + oc * 128:cc * C.D + (oc + 1) * 128],
                                       ctx[:, cc * C.T + qb * C.QB:cc * C.T + (qb + 1) * C.QB],
                                       start=(cc == 0), stop=(cc == C.DC - 1))
                    rec["o"][zi] = kb.inc(ins, "sT")
                for tt in range(NT4):
                    ti = qb * NT4 + tt
                    kb.w(t, "sV", v_zall(qb))
                    if ti >= 2:
                        kb.w(t, "sV", v_ztm(ti - 2))
                    for oc in range(C.DC):
                        base = (qb % 2) * C.DC * C.QB + oc * C.QB + tt * 128
                        ins = t.transpose(pz[ti % 2][:, oc * 128:(oc + 1) * 128],
                                          z_fm[:, base:base + 128], sb["idf"][:, :])
                    rec["tp"][ti] = kb.inc(ins, "sT")

        @blk.scalar
        def _(s):
            for ti in range(NTI):
                sl = ti % 2
                kb.w(s, "sV", v_ztm(ti))
                ins = s.activation(out=sqd2[:, :], in_=z_tm[:, sl * C.D:(sl + 1) * C.D],
                                   func=AF.Square,
                                   accum_out=lns[:, LNS_SSQ0 + sl:LNS_SSQ0 + sl + 1])
                kb.inc(ins, "sS")
                kb.w(s, "sV", v_rinv(ti))
                ins = s.activation(out=lns[:, LNS_R0:LNS_R0 + 1],
                                   in_=lns[:, LNS_RINV:LNS_RINV + 1], func=AF.Sqrt)
                kb.inc(ins, "sS")

        @blk.vector
        def _(v):
            kb.w(v, "sx", b["sx"] + 16)
            for qb in range(C.NQB):
                for oc in range(C.DC):
                    zi = qb * C.DC + oc
                    kb.w(v, "sT", rec["o"][zi])
                    if qb >= 2:
                        kb.w(v, "sT", rec["tp"][(qb - 2) * NT4 + NT4 - 1])
                    base = (qb % 2) * C.DC * C.QB + oc * C.QB
                    ins = v.scalar_tensor_tensor(
                        out=z_fm[:, base:base + C.QB],
                        in0=po[zi % 2][:, 0:C.QB], scalar=sb["bot"][:, oc:oc + 1],
                        in1=xn_fm[:, oc * C.T + qb * C.QB:oc * C.T + (qb + 1) * C.QB],
                        op0=ALU.add, op1=ALU.add)
                    kb.inc(ins, "sV")
                for tt in range(NT4):
                    ti = qb * NT4 + tt
                    sl = ti % 2
                    kb.w(v, "sT", rec["tp"][ti])
                    ins = v.tensor_copy(out=z_tm[:, sl * C.D:(sl + 1) * C.D],
                                        in_=pz[sl][:, 0:C.D])
                    n = kb.inc(ins, "sV")
                    kb.w(v, "sV", n)
                    kb.w(v, "sS", b["sS"] + 2 * ti + 1)
                    ins = _ln_stats(v, kb, lns, LNS_SSQ0 + sl,
                                    z_tm[:, sl * C.D:(sl + 1) * C.D], C.D)
                    kb.inc(ins, "sV")
                    kb.w(v, "sS", b["sS"] + 2 * ti + 2)
                    _ln_newton(v, kb, lns)
                    ins = v.scalar_tensor_tensor(out=t1[:, :], in0=z_tm[:, sl * C.D:(sl + 1) * C.D],
                                                 scalar=lns[:, LNS_MU:LNS_MU + 1],
                                                 in1=gb_sb[:, 0:C.D],
                                                 op0=ALU.subtract, op1=ALU.mult)
                    kb.vsync(v, ins)
                    if ti >= 2:
                        kb.w(v, "sx", b["sx"] + 16 * (1 + (ti - 2) + 1))
                    ins = v.scalar_tensor_tensor(out=ast[:, sl * C.D:(sl + 1) * C.D],
                                                 in0=t1[:, :],
                                                 scalar=lns[:, LNS_RSTD:LNS_RSTD + 1],
                                                 in1=gb_sb[:, C.D:2 * C.D],
                                                 op0=ALU.mult, op1=ALU.add)
                    kb.inc(ins, "sV")


def _p3(nc, kb, C, io, sb, p3st):
    b = dict(kb.c)
    PER_T = 1 + 2 * C.HC + C.DC
    PER_V = 1 + 2 * C.HC + C.DC

    def t_bc(esl):
        return b["sT"] + esl * PER_T + 1

    def t_u(esl, hc):
        return b["sT"] + esl * PER_T + 1 + 2 * hc + 1

    def t_v(esl, hc):
        return b["sT"] + esl * PER_T + 1 + 2 * hc + 2

    def t_y(esl, oc):
        return b["sT"] + esl * PER_T + 1 + 2 * C.HC + oc + 1

    def v_wbc(esl):
        return b["sV"] + esl * PER_V + 1

    def v_tt1(esl, hc):
        return b["sV"] + esl * PER_V + 1 + 2 * hc + 1

    def v_tt2(esl, hc):
        return b["sV"] + esl * PER_V + 1 + 2 * hc + 2

    def v_acc(esl, oc):
        return b["sV"] + esl * PER_V + 1 + 2 * C.HC + oc + 1

    def s_silu(esl, hc):
        return b["sS"] + esl * C.HC + hc + 1

    xn_fm, wT8, acc = sb["xn_fm"], sb["wT8"], sb["acc"]
    rg = {"w12": {}, "w1s": {}, "W3": {}}

    with contextlib.ExitStack() as ps:
        E3 = ps.enter_context
        W3_sb = E3(nc.sbuf_tensor("W3_sb", [128, 3 * C.HC * 128], BF16))
        h_sb = E3(nc.sbuf_tensor("h_sb", [128, C.HC * C.SL], BF16))
        su_sb = E3(nc.sbuf_tensor("su_sb", [128, 2 * C.SL], BF16))
        tb_sb = E3(nc.sbuf_tensor("tb_sb", [128, C.SL], BF16))
        wbc_sb = E3(nc.sbuf_tensor("wbc_sb", [128, C.SL], BF16))
        w1s = E3(nc.sbuf_tensor("w1s", [1, 2 * C.SL], BF16))
        pu = [_psum(nc, ps, f"pu{i}", 128, C.SL, F32) for i in range(2)]
        pvv = [_psum(nc, ps, f"pv3{i}", 128, C.SL, F32) for i in range(2)]
        py = [_psum(nc, ps, f"py{i}", 128, C.SL, F32) for i in range(3)]
        pb = _psum(nc, ps, "pb", 128, C.SL, F32)
        blk = E3(nc.Block("p3"))

        NW3 = 3
        H2 = C.HE // 2

        @blk.gpsimd
        def _(g):
            def load_w1s(esl):
                e, sl = divmod(esl, C.NSL)
                if esl >= 2:
                    kb.w(g, "sT", t_bc(esl - 2))
                n = kb.inc(g.dma_start(out=w1s[0:1, (esl % 2) * C.SL:(esl % 2 + 1) * C.SL],
                                       in_=wT8[e:e + 1, sl * C.SL:(sl + 1) * C.SL]),
                           "sg", 16)
                rg["w1s"][esl] = n

            def load_W3(esl, oc):
                e = esl // C.NSL
                w = esl * C.DC + oc
                if w >= NW3:
                    esl2, oc2 = divmod(w - NW3, C.DC)
                    kb.w(g, "sT", t_y(esl2, oc2))
                n = kb.inc(g.dma_start(
                    out=bass.AP(W3_sb, (w % NW3) * C.HC * 128,
                                [[NW3 * C.HC * 128, 128], [128, C.HC], [1, 128]]),
                    in_=bass.AP(io["W3"].tensor, e * C.HE * C.D + oc * 128,
                                [[C.D, 128], [128 * C.D, C.HC], [1, 128]])), "sg", 16)
                rg["W3"][w] = n

            for e in range(C.NE):
                load_w1s(e * C.NSL)
                for oc in range(min(NW3, C.DC)):
                    load_W3(e * C.NSL, oc)
                for half in range(2):
                    if e > 0:
                        kb.w(g, "sT", t_v((e - 1) * C.NSL + C.NSL - 1,
                                          (half + 1) * (C.HC // 2) - 1))
                    for wn, wsb in (("W1", "W1s"), ("W2", "W2s")):
                        n = kb.inc(g.dma_start(
                            out=bass.AP(sb[wsb], half * H2,
                                        [[C.DC * C.HE, 128], [C.HE, C.DC], [1, H2]]),
                            in_=io[wn][e][:, half * H2:(half + 1) * H2
                                          ].rearrange("(c p) h -> p c h", p=128)),
                            "sg", 16)
                    rg["w12"][(e, half)] = n
                for oc in range(min(NW3, C.DC), C.DC):
                    load_W3(e * C.NSL, oc)
                for sl in range(1, C.NSL):
                    esl = e * C.NSL + sl
                    load_w1s(esl)
                    for oc in range(C.DC):
                        load_W3(esl, oc)

        @blk.tensor
        def _(t):
            for e in range(C.NE):
                for sl in range(C.NSL):
                    esl = e * C.NSL + sl
                    kb.w(t, "sg", rg["w1s"][esl])
                    if esl >= 1:
                        kb.w(t, "sV", v_wbc(esl - 1))
                    ins = t.matmul(pb[:, 0:C.SL], sb["ones"][:, :],
                                   w1s[0:1, (esl % 2) * C.SL:(esl % 2 + 1) * C.SL],
                                   start=True, stop=True)
                    kb.inc(ins, "sT")
                    for hc in range(C.HC):
                        if sl == 0 and hc % (C.HC // 2) == 0:
                            kb.w(t, "sg", rg["w12"][(e, hc // (C.HC // 2))])
                        if hc >= 2:
                            kb.w(t, "sS", s_silu(esl, hc - 2))
                        elif esl >= 1:
                            kb.w(t, "sS", s_silu(esl - 1, C.HC - 2 + hc))
                        for dc in range(C.DC):
                            ins = t.matmul(pu[hc % 2][:, 0:C.SL],
                                           sb["W1s"][:, dc * C.HE + hc * 128:dc * C.HE + (hc + 1) * 128],
                                           xn_fm[:, dc * C.T + sl * C.SL:dc * C.T + (sl + 1) * C.SL],
                                           start=(dc == 0), stop=(dc == C.DC - 1))
                        kb.inc(ins, "sT")
                        if hc >= 2:
                            kb.w(t, "sV", v_tt1(esl, hc - 2))
                        elif esl >= 1:
                            kb.w(t, "sV", v_tt1(esl - 1, C.HC - 2 + hc))
                        for dc in range(C.DC):
                            ins = t.matmul(pvv[hc % 2][:, 0:C.SL],
                                           sb["W2s"][:, dc * C.HE + hc * 128:dc * C.HE + (hc + 1) * 128],
                                           xn_fm[:, dc * C.T + sl * C.SL:dc * C.T + (sl + 1) * C.SL],
                                           start=(dc == 0), stop=(dc == C.DC - 1))
                        kb.inc(ins, "sT")
                    kb.w(t, "sV", v_tt2(esl, C.HC - 1))
                    for oc in range(C.DC):
                        w = esl * C.DC + oc
                        kb.w(t, "sg", rg["W3"][w])
                        if w >= 3:
                            kb.w(t, "sV", v_acc(*divmod(w - 3, C.DC)))
                        for hc in range(C.HC):
                            ins = t.matmul(py[w % 3][:, 0:C.SL],
                                           bass.AP(W3_sb, (w % NW3) * C.HC * 128 + hc * 128,
                                                   [[NW3 * C.HC * 128, 128], [1, 128]]),
                                           h_sb[:, hc * C.SL:(hc + 1) * C.SL],
                                           start=(hc == 0), stop=(hc == C.HC - 1))
                        kb.inc(ins, "sT")
                    if esl == C.NE * C.NSL - 1:
                        ins = t.matmul(pb[:, 0:C.SL], sb["ones"][:, :],
                                       w1s[0:1, (esl % 2) * C.SL:(esl % 2 + 1) * C.SL],
                                       start=True, stop=True)
                        kb.inc(ins, "sT")

        @blk.scalar
        def _(s):
            for esl in range(C.NE * C.NSL):
                for hc in range(C.HC):
                    kb.w(s, "sT", t_u(esl, hc))
                    if hc >= 2:
                        kb.w(s, "sV", v_tt1(esl, hc - 2))
                    elif esl >= 1:
                        kb.w(s, "sV", v_tt1(esl - 1, C.HC - 2 + hc))
                    ins = s.activation(out=su_sb[:, (hc % 2) * C.SL:(hc % 2 + 1) * C.SL],
                                       in_=pu[hc % 2][:, 0:C.SL], func=AF.Silu)
                    kb.inc(ins, "sS")

        @blk.vector
        def _(v):
            for e in range(C.NE):
                for sl in range(C.NSL):
                    esl = e * C.NSL + sl
                    kb.w(v, "sT", t_bc(esl))
                    ins = v.tensor_copy(out=wbc_sb[:, :], in_=pb[:, 0:C.SL])
                    n = kb.inc(ins, "sV")
                    kb.w(v, "sV", n)
                    for hc in range(C.HC):
                        kb.w(v, "sT", t_v(esl, hc))
                        kb.w(v, "sS", s_silu(esl, hc))
                        ins = v.tensor_mul(tb_sb[:, :], pvv[hc % 2][:, 0:C.SL],
                                           su_sb[:, (hc % 2) * C.SL:(hc % 2 + 1) * C.SL])
                        n = kb.inc(ins, "sV")
                        kb.w(v, "sV", n)
                        if hc == 0 and esl >= 1:
                            kb.w(v, "sT", t_y(esl - 1, C.DC - 1))
                        ins = v.tensor_mul(h_sb[:, hc * C.SL:(hc + 1) * C.SL],
                                           tb_sb[:, :], wbc_sb[:, :])
                        if hc == C.HC - 1:
                            kb.vsync(v, ins)
                            v.sem_inc(kb.sem["sV"], 1)
                            kb.c["sV"] += 1
                        else:
                            kb.inc(ins, "sV")
                    for oc in range(C.DC):
                        if oc < C.DC - 1:
                            kb.w(v, "sT", t_y(esl, oc + 1))
                        elif esl < C.NE * C.NSL - 1:
                            kb.w(v, "sT", t_bc(esl + 1))
                        else:
                            kb.w(v, "sT", t_y(esl, oc) + 1)
                        dst = acc[:, oc * C.T + sl * C.SL:oc * C.T + (sl + 1) * C.SL]
                        w = esl * C.DC + oc
                        if e == 0:
                            ins = v.tensor_copy(out=dst, in_=py[w % 3][:, 0:C.SL])
                        else:
                            ins = v.tensor_add(dst, dst, py[w % 3][:, 0:C.SL])
                        kb.inc(ins, "sV")


def _p4(nc, kb, C, io, sb):
    b = dict(kb.c)
    NT = C.NTT
    rec = {"tp": [None] * NT}
    acc = sb["acc"]

    with contextlib.ExitStack() as ps:
        E4 = ps.enter_context
        xin4 = E4(nc.sbuf_tensor("xin4", [128, 2 * C.D], F32))
        atn4 = E4(nc.sbuf_tensor("atn4", [128, 2 * C.D], BF16))
        tmp4 = E4(nc.sbuf_tensor("tmp4", [128, C.D], F32))
        out4 = E4(nc.sbuf_tensor("out4", [128, 2 * C.D], F32))
        pz4 = [_psum(nc, ps, f"pz4{i}", 128, C.D, F32) for i in range(2)]
        blk = E4(nc.Block("p4"))

        @blk.sync
        def _(s):
            for i in range(NT + 1):
                if i < NT:
                    sl = i % 2
                    if i >= 2:
                        kb.w(s, "sV", b["sV"] + (i - 2) + 1)
                    kb.inc(s.dma_start(out=xin4[:, sl * C.D:(sl + 1) * C.D],
                                       in_=io["x_seq"][i * 128:(i + 1) * 128, :]), "sx", 16)
                    kb.inc(s.dma_start(out=atn4[:, sl * C.D:(sl + 1) * C.D],
                                       in_=io["attn_scr"][i * 128:(i + 1) * 128, :]), "sx", 16)
                if i >= 1:
                    j = i - 1
                    kb.w(s, "sV", b["sV"] + j + 1)
                    kb.inc(s.dma_start(out=io["out"][j * 128:(j + 1) * 128, :],
                                       in_=out4[:, (j % 2) * C.D:(j % 2 + 1) * C.D]), "sx", 16)
            s.wait_ge(kb.sem["sx"], kb.c["sx"])

        @blk.tensor
        def _(t):
            for i in range(NT):
                sl = i % 2
                if i >= 2:
                    kb.w(t, "sV", b["sV"] + (i - 2) + 1)
                for oc in range(C.DC):
                    ins = t.transpose(pz4[sl][:, oc * 128:(oc + 1) * 128],
                                      acc[:, oc * C.T + i * 128:oc * C.T + (i + 1) * 128],
                                      sb["idf"][:, :])
                rec["tp"][i] = kb.inc(ins, "sT")

        @blk.vector
        def _(v):
            for i in range(NT):
                sl = i % 2
                kb.w(v, "sT", rec["tp"][i])
                kb.w(v, "sx", b["sx"] + 16 * (2 if i == 0 else 3 * i + 1))
                if i >= 2:
                    kb.w(v, "sx", b["sx"] + 16 * (3 * (i - 2) + 5))
                ins = v.tensor_add(tmp4[:, :], pz4[sl][:, 0:C.D],
                                    xin4[:, sl * C.D:(sl + 1) * C.D])
                kb.vsync(v, ins)
                ins = v.tensor_add(out4[:, sl * C.D:(sl + 1) * C.D], tmp4[:, :],
                                   atn4[:, sl * C.D:(sl + 1) * C.D])
                kb.inc(ins, "sV")


def _aux_inputs(C: Cfg):
    bf = ml_dtypes.bfloat16
    ebc = np.zeros((33, 128), bf)
    ebc[0, 0:64] = 1
    ebc[32, 64:128] = 1
    return {
        "identbf": np.eye(128, dtype=bf),
        "identf32": np.eye(128, dtype=np.float32),
        "ones1": np.ones((1, 128), bf),
        "ebc": ebc,
    }


def _layout_inputs(C: Cfg, inp):
    f = lambda k: np.ascontiguousarray(np.asarray(inp[k], np.float32))
    com = {
        "wq": f("L0_attn_wq"), "wk": f("L0_attn_wk"), "wv": f("L0_attn_wv"),
        "wo": f("L0_attn_wo"), "Wg": f("L0_ffn_Wg"),
        "W1": f("L0_ffn_W1"), "W2": f("L0_ffn_W2"), "W3": f("L0_ffn_W3"),
        "bqt": np.ascontiguousarray(f("L0_attn_bq").reshape(C.DC, 128).T),
        "bkt": np.ascontiguousarray(f("L0_attn_bk").reshape(C.DC, 128).T),
        "bot": np.ascontiguousarray(f("L0_attn_bo").reshape(C.DC, 128).T),
        "gbbc": np.ascontiguousarray(np.concatenate(
            [np.broadcast_to(f("L0_attn_gamma")[None, :], (128, C.D)),
             np.broadcast_to(f("L0_attn_beta")[None, :], (128, C.D))], axis=1)),
        "bvbc": np.ascontiguousarray(np.broadcast_to(f("L0_attn_bv")[None, :], (128, C.D))),
    }
    com.update(_aux_inputs(C))
    return com


def kernel(**inputs):
    C = Cfg()
    nc = build_program(C)
    x = np.asarray(inputs["L0_attn_x"], np.float32)
    B = x.shape[0]
    com = _layout_inputs(C, inputs)
    in_maps = []
    for c in range(2 * B):
        bi, hp = divmod(c, 2)
        xs = x[bi]
        if hp == 1:
            xs = np.concatenate([xs[C.T:], xs[:C.T]], axis=0)
        m = dict(com)
        m["x_seq"] = np.ascontiguousarray(xs)
        in_maps.append(m)
    res = run_bass_kernel_spmd(nc, in_maps, core_ids=list(range(2 * B)))
    out = np.empty((B, C.S, C.D), np.float32)
    for c in range(2 * B):
        bi, hp = divmod(c, 2)
        out[bi, hp * C.T:(hp + 1) * C.T] = res.results[c]["out"]
    return out


# version 15
# speedup vs baseline: 1.0323x; 1.0323x over previous
import contextlib

import numpy as np
import ml_dtypes

import concourse.bass as bass
import concourse.mybir as mybir
from concourse.bass_utils import run_bass_kernel_spmd

F32 = mybir.dt.float32
BF16 = mybir.dt.bfloat16
AF = mybir.ActivationFunctionType
ALU = mybir.AluOpType
AX = mybir.AxisListType

LN_EPS = 1e-5

(LNS_SSUM, LNS_SSQ0, LNS_SSQ1, LNS_MU, LNS_MUSQ, LNS_VEPS, LNS_RINV, LNS_R0,
 LNS_R2, LNS_RSTD, LNS_D01, LNS_D10, LNS_P0, LNS_P1, LNS_V0, LNS_V1) = range(16)


class Cfg:
    def __init__(self, S=4096, D=1024, NH=16, HE=2048, NE=8, NGRP=4, dbg=False,
                 stop_after=None, dbg_taps=None):
        self.S, self.D, self.NH, self.HE, self.NE, self.NGRP, self.dbg = \
            S, D, NH, HE, NE, NGRP, dbg
        self.stop_after = stop_after
        self.dbg_taps = dbg_taps if dbg_taps is not None else {'p0', 'p1p', 'p1d', 'p3d'}
        self.HD = 64
        assert D == NH * self.HD
        self.T = S // 2
        self.DC = D // 128
        self.NKC = S // 128
        self.NTT = self.T // 128
        self.QB = min(512, self.T)
        self.NQB = self.T // self.QB
        self.KBLK = 512
        self.NKB = S // self.KBLK
        self.NKBo = self.T // self.KBLK
        self.HG = NH // NGRP
        self.PHG = self.HG // 2
        self.FH = self.HG * 64
        self.HC = HE // 128
        self.SL = min(512, self.T)
        self.NSL = self.T // self.SL
        self.G = 2
        self.NG = self.NKC // self.G
        assert D % 128 == 0 and self.T % 128 == 0 and S % 512 == 0
        assert self.HG % 2 == 0 and self.NKC % self.G == 0 and self.FH <= 512
        assert self.NTT <= 16


class KB:
    SEMS = ["sx", "sg", "sT", "sS", "sV", "v2", "s2"]

    def __init__(self, nc, st):
        self.nc = nc
        self.sem = {n: st.enter_context(nc.semaphore(n)) for n in self.SEMS}
        self.c = {n: 0 for n in self.SEMS}

    def inc(self, instr, n, k=1):
        instr.then_inc(self.sem[n], k)
        self.c[n] += k
        return self.c[n]

    def w(self, eng, n, v):
        if v > 0:
            eng.wait_ge(self.sem[n], v)

    def vsync(self, eng, ins, sem="v2"):
        n = self.inc(ins, sem)
        eng.wait_ge(self.sem[sem], n)


def _psum(nc, st, name, p, f, dt):
    per = 512 if dt == F32 else 1024
    fpad = ((f + per - 1) // per) * per
    t = st.enter_context(nc.psum_tensor(name, [p, fpad], dt))
    return t


def _ln_stats(v, kb, lns, ssq_c, x_ap, n_feat):
    ins = v.reduce_sum(lns[:, LNS_SSUM:LNS_SSUM + 1], x_ap, axis=AX.X)
    kb.vsync(v, ins)
    ins = v.tensor_scalar_mul(out=lns[:, LNS_MU:LNS_MU + 1],
                              in0=lns[:, LNS_SSUM:LNS_SSUM + 1], scalar1=1.0 / n_feat)
    kb.vsync(v, ins)
    ins = v.tensor_mul(lns[:, LNS_MUSQ:LNS_MUSQ + 1], lns[:, LNS_MU:LNS_MU + 1],
                       lns[:, LNS_MU:LNS_MU + 1])
    kb.vsync(v, ins)
    ins = v.scalar_tensor_tensor(out=lns[:, LNS_VEPS:LNS_VEPS + 1],
                                 in0=lns[:, ssq_c:ssq_c + 1], scalar=1.0 / n_feat,
                                 in1=lns[:, LNS_MUSQ:LNS_MUSQ + 1],
                                 op0=ALU.mult, op1=ALU.subtract)
    kb.vsync(v, ins)
    ins = v.tensor_scalar_add(out=lns[:, LNS_VEPS:LNS_VEPS + 1],
                              in0=lns[:, LNS_VEPS:LNS_VEPS + 1], scalar1=LN_EPS)
    kb.vsync(v, ins)
    return v.reciprocal(lns[:, LNS_RINV:LNS_RINV + 1], lns[:, LNS_VEPS:LNS_VEPS + 1])


def _ln_newton(v, kb, lns):
    ins = v.tensor_mul(lns[:, LNS_R2:LNS_R2 + 1], lns[:, LNS_R0:LNS_R0 + 1],
                       lns[:, LNS_R0:LNS_R0 + 1])
    kb.vsync(v, ins)
    ins = v.tensor_single_scalar(out=lns[:, LNS_R2:LNS_R2 + 1],
                                 in_=lns[:, LNS_R2:LNS_R2 + 1],
                                 scalar=lns[:, LNS_VEPS:LNS_VEPS + 1], op=ALU.mult)
    kb.vsync(v, ins)
    ins = v.tensor_scalar(out=lns[:, LNS_R2:LNS_R2 + 1], in0=lns[:, LNS_R2:LNS_R2 + 1],
                          scalar1=-0.5, scalar2=1.5, op0=ALU.mult, op1=ALU.add)
    kb.vsync(v, ins)
    ins = v.tensor_mul(lns[:, LNS_RSTD:LNS_RSTD + 1], lns[:, LNS_R0:LNS_R0 + 1],
                       lns[:, LNS_R2:LNS_R2 + 1])
    kb.vsync(v, ins)


RACE_DETECT = False


def build_program(C: Cfg):
    nc = bass.Bass("TRN2", target_bir_lowering=False, debug=False,
                   detect_race_conditions=RACE_DETECT)
    dt = nc.dram_tensor
    io = {}
    io["x_seq"] = dt("x_seq", [C.S, C.D], F32, kind="ExternalInput").ap()
    for n in ("wq", "wk", "wv", "wo"):
        io[n] = dt(n, [C.D, C.D], F32, kind="ExternalInput").ap()
    io["Wg"] = dt("Wg", [C.D, C.NE], F32, kind="ExternalInput").ap()
    io["W1"] = dt("W1", [C.NE, C.D, C.HE], F32, kind="ExternalInput").ap()
    io["W2"] = dt("W2", [C.NE, C.D, C.HE], F32, kind="ExternalInput").ap()
    io["W3"] = dt("W3", [C.NE, C.HE, C.D], F32, kind="ExternalInput").ap()
    io["identbf"] = dt("identbf", [128, 128], BF16, kind="ExternalInput").ap()
    io["identf32"] = dt("identf32", [128, 128], F32, kind="ExternalInput").ap()
    io["ones1"] = dt("ones1", [1, 128], BF16, kind="ExternalInput").ap()
    io["ebc"] = dt("ebc", [33, 128], BF16, kind="ExternalInput").ap()
    io["bqt"] = dt("bqt", [128, C.DC], F32, kind="ExternalInput").ap()
    io["bkt"] = dt("bkt", [128, C.DC], F32, kind="ExternalInput").ap()
    io["bot"] = dt("bot", [128, C.DC], F32, kind="ExternalInput").ap()
    io["gbbc"] = dt("gbbc", [128, 2 * C.D], F32, kind="ExternalInput").ap()
    io["bvbc"] = dt("bvbc", [128, C.D], F32, kind="ExternalInput").ap()
    io["attn_scr"] = dt("attn_scr", [C.T, C.D], BF16, kind="Internal").ap()
    io["out"] = dt("out", [C.T, C.D], F32, kind="ExternalOutput").ap()
    if C.dbg:
        io["dbg_xnfm"] = dt("dbg_xnfm", [128, C.DC * C.T], BF16, kind="ExternalOutput").ap()
        io["dbg_lns"] = dt("dbg_lns", [128, 16], F32, kind="ExternalOutput").ap()
        io["dbg_xntm"] = dt("dbg_xntm", [128, 2 * C.D], BF16, kind="ExternalOutput").ap()
        io["dbg_xin"] = dt("dbg_xin", [128, 2 * C.D], F32, kind="ExternalOutput").ap()
        io["dbg_wT8"] = dt("dbg_wT8", [8, C.T], F32, kind="ExternalOutput").ap()
        io["dbg_K"] = dt("dbg_K", [C.NGRP, 128, C.PHG * C.S], BF16, kind="ExternalOutput").ap()
        io["dbg_V"] = dt("dbg_V", [C.NGRP, 128, C.NKC * C.HG * 65], BF16, kind="ExternalOutput").ap()
        io["dbg_Q"] = dt("dbg_Q", [C.NGRP, 128, C.PHG * C.T], BF16, kind="ExternalOutput").ap()
        io["dbg_ctx"] = dt("dbg_ctx", [128, (C.NH // 2) * C.T], BF16, kind="ExternalOutput").ap()
        io["dbg_acc"] = dt("dbg_acc", [128, C.DC * C.T], F32, kind="ExternalOutput").ap()

    with contextlib.ExitStack() as st:
        kb = KB(nc, st)
        E = st.enter_context
        sb = {}
        sb["xn_fm"] = E(nc.sbuf_tensor("xn_fm", [128, C.DC * C.T], BF16))
        sb["wT8"] = E(nc.sbuf_tensor("wT8", [8, C.T], F32))
        sb["idb"] = E(nc.sbuf_tensor("idb", [128, 128], BF16))
        sb["idf"] = E(nc.sbuf_tensor("idf", [128, 128], F32))
        sb["ones"] = E(nc.sbuf_tensor("ones", [1, 128], BF16))
        sb["ebc"] = E(nc.sbuf_tensor("ebc_s", [33, 128], BF16))
        sb["bqt"] = E(nc.sbuf_tensor("bqt_s", [128, C.DC], F32))
        sb["bkt"] = E(nc.sbuf_tensor("bkt_s", [128, C.DC], F32))
        sb["bot"] = E(nc.sbuf_tensor("bot_s", [128, C.DC], F32))
        sb["Wg"] = E(nc.sbuf_tensor("Wgs", [128, C.DC * C.NE], BF16))
        sb["lns"] = E(nc.sbuf_tensor("lns", [128, 16], F32))
        sb["w_tm"] = E(nc.sbuf_tensor("w_tm", [128, C.NTT * C.NE], F32))
        sb["gsc"] = E(nc.sbuf_tensor("gsc", [128, 4 * C.NE], F32))
        sb["z8"] = E(nc.sbuf_tensor("z8", [128, 8], F32))
        sb["n8"] = E(nc.sbuf_tensor("n8", [128, 8], F32))

        with contextlib.ExitStack() as p02:
            sb["xn_oth"] = p02.enter_context(
                nc.sbuf_tensor("xn_oth", [128, C.DC * C.T], BF16))
            _p0(nc, kb, C, io, sb, st)

            with contextlib.ExitStack() as p1s:
                sb["ctx"] = p1s.enter_context(
                    nc.sbuf_tensor("ctx", [128, (C.NH // 2) * C.T], BF16))
                with contextlib.ExitStack() as p1o:
                    E1 = p1o.enter_context
                    sb["wq_g"] = E1(nc.sbuf_tensor("wq_g", [128, C.DC * C.FH], BF16))
                    sb["wk_g"] = E1(nc.sbuf_tensor("wk_g", [128, C.DC * C.FH], BF16))
                    sb["wv_g"] = E1(nc.sbuf_tensor("wv_g", [128, C.DC * C.FH], BF16))
                    sb["K"] = E1(nc.sbuf_tensor("Ksb", [128, C.PHG * C.S], BF16))
                    sb["V"] = E1(nc.sbuf_tensor("Vsb", [128, C.NKC * C.HG * 65], BF16))
                    sb["Q"] = E1(nc.sbuf_tensor("Qsb", [128, C.PHG * C.T], BF16))
                    sb["p_sb"] = E1(nc.sbuf_tensor("p_sb", [128, 4 * C.G * C.QB], BF16))
                    sb["rZ2"] = E1(nc.sbuf_tensor("rZ2", [33, C.QB], BF16))
                    sb["bc_sb"] = E1(nc.sbuf_tensor("bc_sb", [128, C.QB], BF16))
                    sb["bvbc"] = E1(nc.sbuf_tensor("bvbcs", [128, C.D], F32))
                    for grp in range(C.NGRP if C.stop_after != "p0" else 0):
                        _p1_proj(nc, kb, C, grp, io, sb)
                        if C.stop_after == "p1proj" and grp == 0:
                            break
                        _p1_attn(nc, kb, C, grp, io, sb)
                    if C.stop_after in ("p0", "p1proj", "p1"):
                        return nc
                    if C.dbg and 'p1d' in C.dbg_taps:
                        with nc.Block("p1d") as blk:
                            @blk.gpsimd
                            def _(g):
                                kb.inc(g.dma_start(out=io["dbg_ctx"], in_=sb["ctx"][:, :]), "sg", 16)
                                g.wait_ge(kb.sem["sg"], kb.c["sg"])
                _p2(nc, kb, C, io, sb)
        with contextlib.ExitStack() as p3s:
            E3 = p3s.enter_context
            sb["acc"] = E3(nc.sbuf_tensor("acc", [128, C.DC * C.T], F32))
            sb["W1s"] = E3(nc.sbuf_tensor("W1s", [128, C.DC * C.HE], BF16))
            sb["W2s"] = E3(nc.sbuf_tensor("W2s", [128, C.DC * C.HE], BF16))
            _p3(nc, kb, C, io, sb, p3s)
            if C.dbg and 'p3d' in C.dbg_taps:
                with nc.Block("p3d") as blk:
                    @blk.gpsimd
                    def _(g):
                        kb.inc(g.dma_start(out=io["dbg_acc"], in_=sb["acc"][:, :]), "sg", 16)
                        g.wait_ge(kb.sem["sg"], kb.c["sg"])
            _p4(nc, kb, C, io, sb)
    return nc


def _p0(nc, kb, C, io, sb, st_outer):
    b = dict(kb.c)
    NT = C.NTT
    NT2 = 2 * NT
    NCONST = 7
    rec = {"tp": [None] * NT2, "lg": None, "lgt": [None] * NT, "wt": [None] * NT}
    xn_fm, lns, gsc, w_tm, wT8 = sb["xn_fm"], sb["lns"], sb["gsc"], sb["w_tm"], sb["wT8"]
    BV = 3 * NT + 2 + 2 * NT + 1

    def v_rinv(i):
        return b["sV"] + (3 * i + 1 if i < NT else BV + 3 * (i - NT) + 1)

    def v_xntm(i):
        return v_rinv(i) + 1

    def v_evac(i):
        return v_rinv(i) + 2

    def s_sq(i):
        return b["sS"] + (2 * i + 1 if i < NT else 3 * NT + 2 * (i - NT) + 1)

    def s_r0(i):
        return s_sq(i) + 1

    with contextlib.ExitStack() as ps:
        E0 = ps.enter_context
        xin = E0(nc.sbuf_tensor("xin", [128, 2 * C.D], F32))
        xntm = E0(nc.sbuf_tensor("xntm", [128, 2 * C.D], BF16))
        sqd = E0(nc.sbuf_tensor("sqd", [128, C.D], BF16))
        lgf = E0(nc.sbuf_tensor("lgf", [8, C.T], F32))
        lgtm = E0(nc.sbuf_tensor("lgtm", [128, C.NTT * C.NE], F32))
        pst = [_psum(nc, ps, f"pst{i}", 128, C.D, BF16) for i in range(2)]
        ps_g = _psum(nc, ps, "ps_g", 8, C.T, F32)
        ps_t = _psum(nc, ps, "ps_t", 128, NT * C.NE, F32)
        blk = E0(nc.Block("p0"))

        def ln_dst(i):
            if i < NT:
                return bass.AP(xn_fm, i * 128, [[C.DC * C.T, 128], [C.T, C.DC], [1, 128]])
            return bass.AP(sb["xn_oth"], (i - NT) * 128,
                           [[C.DC * C.T, 128], [C.T, C.DC], [1, 128]])

        @blk.gpsimd
        def _(g):
            kb.inc(g.dma_start(
                out=bass.AP(sb["Wg"], 0, [[C.DC * C.NE, 128], [C.NE, C.DC], [1, C.NE]]),
                in_=io["Wg"].rearrange("(c p) e -> p c e", p=128)), "sg", 16)

        @blk.sync
        def _(s):
            for name in ("identbf", "identf32", "ones1", "ebc", "bqt", "bkt", "bot"):
                tgt = {"identbf": "idb", "identf32": "idf", "ones1": "ones",
                       "ebc": "ebc", "bqt": "bqt", "bkt": "bkt", "bot": "bot"}[name]
                kb.inc(s.dma_start(out=sb[tgt][:, :], in_=io[name]), "sx", 16)
            for i in range(NT2):
                if i >= 2:
                    kb.w(s, "sV", v_xntm(i - 2))
                kb.inc(s.dma_start(out=xin[:, (i % 2) * C.D:(i % 2 + 1) * C.D],
                                   in_=io["x_seq"][i * 128:(i + 1) * 128, :]), "sx", 16)

        def emit_ln_tensor(t, i):
            sl = i % 2
            kb.w(t, "sV", v_xntm(i))
            if i >= 2:
                kb.w(t, "sV", v_evac(i - 2))
            for dc in range(C.DC):
                ins = t.transpose(pst[sl][:, dc * 128:(dc + 1) * 128],
                                  xntm[:, sl * C.D + dc * 128:sl * C.D + (dc + 1) * 128],
                                  sb["idb"][:, :])
            rec["tp"][i] = kb.inc(ins, "sT")

        @blk.tensor
        def _(t):
            for i in range(NT):
                emit_ln_tensor(t, i)
            kb.w(t, "sg", b["sg"] + 16)
            kb.w(t, "sV", b["sV"] + 3 * NT)
            for qb in range(C.T // 512):
                for dc in range(C.DC):
                    ins = t.matmul(ps_g[0:C.NE, qb * 512:(qb + 1) * 512],
                                   bass.AP(sb["Wg"], dc * C.NE, [[C.DC * C.NE, 128], [1, C.NE]]),
                                   xn_fm[:, dc * C.T + qb * 512:dc * C.T + (qb + 1) * 512],
                                   start=(dc == 0), stop=(dc == C.DC - 1))
            rec["lg"] = kb.inc(ins, "sT")
            kb.w(t, "sV", b["sV"] + 3 * NT + 1)
            for i in range(NT):
                ins = t.transpose(ps_t[:, i * C.NE:i * C.NE + C.NE],
                                  lgf[0:8, i * 128:(i + 1) * 128], sb["idf"][0:8, 0:8])
                rec["lgt"][i] = kb.inc(ins, "sT")
            for i in range(NT):
                kb.w(t, "sV", b["sV"] + 3 * NT + 2 + 2 * i + 2)
                ins = t.transpose(ps_g[0:C.NE, i * 128:(i + 1) * 128],
                                  w_tm[:, i * C.NE:(i + 1) * C.NE], sb["idf"][:, :])
                rec["wt"][i] = kb.inc(ins, "sT")
            for i in range(NT, NT2):
                emit_ln_tensor(t, i)

        def emit_ln_scalar(s, i):
            sl = i % 2
            kb.w(s, "sx", b["sx"] + 16 * (NCONST + i + 1))
            if i >= 2:
                kb.w(s, "sV", v_rinv(i - 2))
            ins = s.activation(out=sqd[:, :], in_=xin[:, sl * C.D:(sl + 1) * C.D],
                               func=AF.Square,
                               accum_out=lns[:, LNS_SSQ0 + sl:LNS_SSQ0 + sl + 1])
            kb.inc(ins, "sS")
            kb.w(s, "sV", v_rinv(i))
            ins = s.activation(out=lns[:, LNS_R0:LNS_R0 + 1],
                               in_=lns[:, LNS_RINV:LNS_RINV + 1], func=AF.Sqrt)
            kb.inc(ins, "sS")

        @blk.scalar
        def _(s):
            for i in range(NT):
                emit_ln_scalar(s, i)
            for i in range(NT):
                kb.w(s, "sV", b["sV"] + 3 * NT + 2 + 2 * i + 1)
                s.activation(out=lns[:, LNS_P0:LNS_P0 + 1],
                             in_=lns[:, LNS_D01:LNS_D01 + 1], func=AF.Sigmoid)
                ins = s.activation(out=lns[:, LNS_P1:LNS_P1 + 1],
                                   in_=lns[:, LNS_D10:LNS_D10 + 1], func=AF.Sigmoid)
                kb.inc(ins, "sS")
            for i in range(NT, NT2):
                emit_ln_scalar(s, i)

        def emit_ln_vector(v, i):
            sl = i % 2
            kb.w(v, "sx", b["sx"] + 16 * (NCONST + i + 1))
            kb.w(v, "sS", s_sq(i))
            ins = _ln_stats(v, kb, lns, LNS_SSQ0 + sl, xin[:, sl * C.D:(sl + 1) * C.D], C.D)
            kb.inc(ins, "sV")
            kb.w(v, "sS", s_r0(i))
            _ln_newton(v, kb, lns)
            if i >= 2:
                kb.w(v, "sT", rec["tp"][i - 2])
            ins = v.tensor_scalar(out=xntm[:, sl * C.D:(sl + 1) * C.D],
                                  in0=xin[:, sl * C.D:(sl + 1) * C.D],
                                  scalar1=lns[:, LNS_MU:LNS_MU + 1],
                                  scalar2=lns[:, LNS_RSTD:LNS_RSTD + 1],
                                  op0=ALU.subtract, op1=ALU.mult)
            kb.inc(ins, "sV")
            kb.w(v, "sT", rec["tp"][i])
            ins = v.tensor_copy(out=ln_dst(i), in_=pst[sl][:, 0:C.D])
            kb.inc(ins, "sV")

        @blk.vector
        def _(v):
            for i in range(NT):
                emit_ln_vector(v, i)
            kb.w(v, "sT", rec["lg"])
            ins = v.tensor_copy(out=lgf[0:8, :], in_=ps_g[0:8, 0:C.T])
            kb.inc(ins, "sV")
            kb.w(v, "sT", rec["lgt"][NT - 1])
            ins = v.tensor_copy(out=lgtm[:, :], in_=ps_t[:, 0:NT * C.NE])
            kb.vsync(v, ins)
            ins = v.memset(sb["z8"][:, :], 0.0)
            kb.inc(ins, "sV")
            ins = v.memset(sb["z8"][:, :], 0.0)
            kb.vsync(v, ins)
            for i in range(NT):
                lg = lgtm[:, i * C.NE:(i + 1) * C.NE]
                ins = v.reduce_max(lns[:, LNS_V0:LNS_V0 + 1], lg, axis=AX.X)
                kb.vsync(v, ins)
                ins = v.tensor_scalar_add(out=sb["n8"][:, :], in0=sb["z8"][:, :],
                                          scalar1=lns[:, LNS_V0:LNS_V0 + 1])
                kb.vsync(v, ins)
                ins = v.match_replace(out=gsc[:, 0:C.NE], in_to_replace=sb["n8"][:, 0:8],
                                      in_values=lg, imm_value=-1e30)
                kb.vsync(v, ins)
                ins = v.reduce_max(lns[:, LNS_V1:LNS_V1 + 1], gsc[:, 0:C.NE], axis=AX.X)
                kb.vsync(v, ins)
                ins = v.tensor_scalar_add(out=sb["n8"][:, :], in0=sb["z8"][:, :],
                                          scalar1=lns[:, LNS_V1:LNS_V1 + 1])
                kb.vsync(v, ins)
                ins = v.match_replace(out=gsc[:, C.NE:2 * C.NE], in_to_replace=sb["n8"][:, 0:8],
                                      in_values=gsc[:, 0:C.NE], imm_value=-1e30)
                kb.vsync(v, ins)
                v.tensor_tensor(out=gsc[:, 2 * C.NE:3 * C.NE], in0=lg,
                                in1=gsc[:, 0:C.NE], op=ALU.not_equal)
                ins = v.tensor_tensor(out=gsc[:, 3 * C.NE:4 * C.NE], in0=gsc[:, 0:C.NE],
                                      in1=gsc[:, C.NE:2 * C.NE], op=ALU.not_equal)
                kb.vsync(v, ins)
                ins = v.tensor_sub(lns[:, LNS_D01:LNS_D01 + 1], lns[:, LNS_V0:LNS_V0 + 1],
                                   lns[:, LNS_V1:LNS_V1 + 1])
                kb.vsync(v, ins)
                ins = v.tensor_scalar_mul(out=lns[:, LNS_D10:LNS_D10 + 1],
                                          in0=lns[:, LNS_D01:LNS_D01 + 1], scalar1=-1.0)
                kb.inc(ins, "sV")
                kb.w(v, "sS", b["sS"] + 2 * NT + i + 1)
                ins = v.tensor_single_scalar(out=gsc[:, 2 * C.NE:3 * C.NE],
                                             in_=gsc[:, 2 * C.NE:3 * C.NE],
                                             scalar=lns[:, LNS_P0:LNS_P0 + 1], op=ALU.mult)
                kb.vsync(v, ins)
                ins = v.scalar_tensor_tensor(out=w_tm[:, i * C.NE:(i + 1) * C.NE],
                                             in0=gsc[:, 3 * C.NE:4 * C.NE],
                                             scalar=lns[:, LNS_P1:LNS_P1 + 1],
                                             in1=gsc[:, 2 * C.NE:3 * C.NE],
                                             op0=ALU.mult, op1=ALU.add)
                kb.inc(ins, "sV")
            kb.w(v, "sT", rec["wt"][NT - 1])
            ins = v.tensor_copy(out=wT8[0:8, :], in_=ps_g[0:8, 0:C.T])
            kb.inc(ins, "sV")
            for i in range(NT, NT2):
                emit_ln_vector(v, i)

        if C.dbg and 'p0' in C.dbg_taps:
            @blk.gpsimd
            def _(g):
                g.wait_ge(kb.sem["sV"], kb.c["sV"])
                g.wait_ge(kb.sem["sS"], kb.c["sS"])
                kb.inc(g.dma_start(out=io["dbg_xnfm"], in_=xn_fm[:, :]), "sg", 16)
                kb.inc(g.dma_start(out=io["dbg_wT8"], in_=wT8[0:8, :]), "sg", 16)
                kb.inc(g.dma_start(out=io["dbg_lns"], in_=lns[:, :]), "sg", 16)
                kb.inc(g.dma_start(out=io["dbg_xntm"], in_=xntm[:, :]), "sg", 16)
                kb.inc(g.dma_start(out=io["dbg_xin"], in_=xin[:, :]), "sg", 16)
                g.wait_ge(kb.sem["sg"], kb.c["sg"])


def _p1_proj(nc, kb, C, grp, io, sb):
    b = dict(kb.c)
    NQ = C.PHG * C.NQB
    NV = C.NKB * 4
    NKp = C.NKB * C.PHG
    VW = C.HG * 65
    xn_fm, xn_oth = sb["xn_fm"], sb["xn_oth"]
    K_sb, V_sb, Q_sb = sb["K"], sb["V"], sb["Q"]

    def xsrc(dc, t0, nn):
        if t0 < C.T:
            return xn_fm[:, dc * C.T + t0:dc * C.T + t0 + nn]
        return xn_oth[:, dc * C.T + t0 - C.T:dc * C.T + t0 - C.T + nn]

    rec = {"qps": [None] * NQ, "kps": [None] * NKp, "vps": [None] * NV}

    def s_ev(i):
        return b["sS"] + i + 1

    with contextlib.ExitStack() as ps:
        pa = [_psum(nc, ps, f"pa{grp}_{i}", 128, max(C.QB, C.KBLK), F32) for i in range(2)]
        pv = [_psum(nc, ps, f"pvv{grp}_{i}", 128, C.FH, F32) for i in range(2)]
        blk = ps.enter_context(nc.Block(f"p1p{grp}"))

        @blk.gpsimd
        def _(g):
            for wn, wsb in (("wq", "wq_g"), ("wk", "wk_g"), ("wv", "wv_g")):
                src = io[wn][:, grp * C.FH:(grp + 1) * C.FH].rearrange("(c p) f -> p c f", p=128)
                kb.inc(g.dma_start(
                    out=bass.AP(sb[wsb], 0, [[C.DC * C.FH, 128], [C.FH, C.DC], [1, C.FH]]),
                    in_=src), "sg", 16)
            if grp == 0:
                kb.inc(g.dma_start(out=sb["bvbc"][:, :], in_=io["bvbc"]), "sg", 16)

        @blk.tensor
        def _(t):
            kb.w(t, "sg", b["sg"] + 16)
            a_idx = 0
            for pp in range(C.PHG):
                for qb in range(C.NQB):
                    sl = pa[a_idx % 2]
                    if a_idx >= 2:
                        kb.w(t, "sS", s_ev(a_idx - 2))
                    for dc in range(C.DC):
                        ins = t.matmul(sl[:, 0:C.QB],
                                       sb["wq_g"][:, dc * C.FH + pp * 128:dc * C.FH + (pp + 1) * 128],
                                       xn_fm[:, dc * C.T + qb * C.QB:dc * C.T + (qb + 1) * C.QB],
                                       start=(dc == 0), stop=(dc == C.DC - 1))
                    rec["qps"][a_idx] = kb.inc(ins, "sT")
                    a_idx += 1
            kb.w(t, "sg", b["sg"] + 48)
            k_idx = 0
            v_idx = 0
            for j in range(C.NKB):
                for pp in range(C.PHG):
                    sl = pa[a_idx % 2]
                    if a_idx >= 2:
                        kb.w(t, "sS", s_ev(a_idx - 2))
                    for dc in range(C.DC):
                        ins = t.matmul(sl[:, 0:C.KBLK],
                                       sb["wk_g"][:, dc * C.FH + pp * 128:dc * C.FH + (pp + 1) * 128],
                                       xsrc(dc, j * C.KBLK, C.KBLK),
                                       start=(dc == 0), stop=(dc == C.DC - 1))
                    rec["kps"][k_idx] = kb.inc(ins, "sT")
                    a_idx += 1
                    k_idx += 1
                for tt in range(4):
                    sl = pv[v_idx % 2]
                    if v_idx >= 2:
                        kb.w(t, "sV", b["sV"] + (v_idx - 2) + 1)
                    for dc in range(C.DC):
                        ins = t.matmul(sl[:, 0:C.FH],
                                       xsrc(dc, j * C.KBLK + tt * 128, 128),
                                       sb["wv_g"][:, dc * C.FH:(dc + 1) * C.FH],
                                       start=(dc == 0), stop=(dc == C.DC - 1))
                    rec["vps"][v_idx] = kb.inc(ins, "sT")
                    v_idx += 1

        @blk.scalar
        def _(s):
            for i in range(NQ):
                pp, qb = divmod(i, C.NQB)
                pg = grp * C.PHG + pp
                kb.w(s, "sT", rec["qps"][i])
                ins = s.activation(
                    out=Q_sb[:, pp * C.T + qb * C.QB:pp * C.T + (qb + 1) * C.QB],
                    in_=pa[i % 2][:, 0:C.QB], func=AF.Identity,
                    bias=sb["bqt"][:, pg:pg + 1])
                kb.inc(ins, "sS")
            for kk in range(NKp):
                bi, pp = divmod(kk, C.PHG)
                pg = grp * C.PHG + pp
                kb.w(s, "sT", rec["kps"][kk])
                ins = s.activation(
                    out=K_sb[:, pp * C.S + bi * C.KBLK:pp * C.S + (bi + 1) * C.KBLK],
                    in_=pa[(NQ + kk) % 2][:, 0:C.KBLK], func=AF.Identity,
                    bias=sb["bkt"][:, pg:pg + 1])
                kb.inc(ins, "sS")

        @blk.vector
        def _(v):
            if grp == 0:
                v.memset(bass.AP(V_sb, 64, [[C.NKC * VW, 128], [65, C.NKC * C.HG], [1, 1]]), 1.0)
                v.memset(sb["rZ2"][:, :], 0.0)
            for v_idx in range(NV):
                kb.w(v, "sT", rec["vps"][v_idx])
                if grp == 0:
                    kb.w(v, "sg", b["sg"] + 64)
                kt = v_idx
                ins = v.scalar_tensor_tensor(
                    out=bass.AP(V_sb, kt * VW, [[C.NKC * VW, 128], [65, C.HG], [1, 64]]),
                    in0=pv[v_idx % 2][:, 0:C.FH], scalar=1.0,
                    in1=sb["bvbc"][:, grp * C.FH:(grp + 1) * C.FH],
                    op0=ALU.mult, op1=ALU.add)
                kb.inc(ins, "sV")

    if C.dbg and 'p1p' in C.dbg_taps:
        with nc.Block(f"p1pd{grp}") as blk:
            @blk.gpsimd
            def _(g):
                kb.inc(g.dma_start(out=io["dbg_K"][grp], in_=K_sb[:, :]), "sg", 16)
                kb.inc(g.dma_start(out=io["dbg_V"][grp], in_=V_sb[:, :]), "sg", 16)
                kb.inc(g.dma_start(out=io["dbg_Q"][grp], in_=Q_sb[:, :]), "sg", 16)
                g.wait_ge(kb.sem["sg"], kb.c["sg"])

def _p1_attn(nc, kb, C, grp, io, sb):
    b = dict(kb.c)
    VW = C.HG * 65
    K_sb, V_sb, Q_sb = sb["K"], sb["V"], sb["Q"]
    p_sb, rZ2, ctx = sb["p_sb"], sb["rZ2"], sb["ctx"]
    NPQ = C.PHG * C.NQB
    NGG = NPQ * 2 * C.NG
    rec = {"s": [None] * NGG, "av": [None] * NGG, "bc": [None] * NPQ}

    with contextlib.ExitStack() as ps:
        ps_s = [_psum(nc, ps, f"pss{grp}_{i}", 128, C.G * C.QB, F32) for i in range(2)]
        ps_ctx = [_psum(nc, ps, f"psc{grp}_{i}", 128, C.QB, F32) for i in range(2)]
        ps_bc = _psum(nc, ps, f"psbc{grp}", 128, C.QB, F32)
        blk = ps.enter_context(nc.Block(f"p1a{grp}"))

        its = []
        for p in range(C.PHG):
            for qb in range(C.NQB):
                for kc in range(C.NKC):
                    its.append((p, qb, p * C.NQB + qb, kc))
        NIT = len(its)
        rec["s"] = [None] * NIT
        rec["av"] = [None] * NIT

        @blk.tensor
        def _(t):
            def emit_S2(it):
                p, qb, i_pq, kc = its[it]
                sl = it % 2
                if it >= 2:
                    kb.w(t, "sS", b["sS"] + it - 1)
                for hl in range(2):
                    ins = t.matmul(
                        ps_s[sl][:, hl * C.QB:(hl + 1) * C.QB],
                        K_sb[hl * 64:hl * 64 + 64,
                             p * C.S + kc * 128:p * C.S + (kc + 1) * 128],
                        Q_sb[hl * 64:hl * 64 + 64,
                             p * C.T + qb * C.QB:p * C.T + (qb + 1) * C.QB],
                        start=True, stop=True)
                rec["s"][it] = kb.inc(ins, "sT")

            def emit_AV2(it):
                p, qb, i_pq, kc = its[it]
                sl = it % 2
                kb.w(t, "sS", b["sS"] + it + 1)
                if kc == 0:
                    kb.w(t, "sV", b["sV"] + 2 * i_pq)
                for hl in range(2):
                    lh = p * 2 + hl
                    ins = t.matmul(
                        ps_ctx[hl][0:65, :],
                        V_sb[:, kc * VW + lh * 65:kc * VW + (lh + 1) * 65],
                        p_sb[:, ((it % 4) * 2 + hl) * C.QB:((it % 4) * 2 + hl + 1) * C.QB],
                        start=(kc == 0), stop=(kc == C.NKC - 1))
                rec["av"][it] = kb.inc(ins, "sT")

            emit_S2(0)
            for it in range(NIT):
                if it + 1 < NIT:
                    emit_S2(it + 1)
                emit_AV2(it)
                p, qb, i_pq, kc = its[it]
                if kc == C.NKC - 1:
                    kb.w(t, "sV", b["sV"] + 2 * i_pq + 1)
                    ins = t.matmul(ps_bc[:, 0:C.QB], sb["ebc"][0:33, :],
                                   rZ2[0:33, 0:C.QB], start=True, stop=True)
                    rec["bc"][i_pq] = kb.inc(ins, "sT")

        @blk.scalar
        def _(s):
            for it in range(NIT):
                sl = it % 2
                kb.w(s, "sT", rec["s"][it])
                if it >= 4:
                    kb.w(s, "sT", rec["av"][it - 4])
                ins = s.activation(out=p_sb[:, (it % 4) * 2 * C.QB:(it % 4 + 1) * 2 * C.QB],
                                   in_=ps_s[sl][:, 0:2 * C.QB], func=AF.Exp, scale=0.125)
                kb.inc(ins, "sS")

        @blk.vector
        def _(v):
            for p in range(C.PHG):
                for qb in range(C.NQB):
                    i_pq = p * C.NQB + qb
                    GG_last = (i_pq + 1) * C.NKC - 1
                    kb.w(v, "sT", rec["av"][GG_last])
                    with nc.allow_low_precision(reason="1/Z feeds a bf16 broadcast matmul"):
                        v.reciprocal(rZ2[0:1, 0:C.QB], ps_ctx[0][64:65, 0:C.QB])
                        ins = v.reciprocal(rZ2[32:33, 0:C.QB], ps_ctx[1][64:65, 0:C.QB])
                    kb.inc(ins, "sV")
                    kb.w(v, "sT", rec["bc"][i_pq])
                    pg = grp * C.PHG + p
                    with nc.allow_low_precision(reason="1/Z bcast is a bf16 scale factor"):
                        ins = v.tensor_copy(out=sb["bc_sb"][:, 0:C.QB], in_=ps_bc[:, 0:C.QB])
                    kb.vsync(v, ins)
                    v.tensor_mul(ctx[0:64, pg * C.T + qb * C.QB:pg * C.T + (qb + 1) * C.QB],
                                 ps_ctx[0][0:64, 0:C.QB], sb["bc_sb"][0:64, 0:C.QB])
                    ins = v.tensor_mul(
                        ctx[64:128, pg * C.T + qb * C.QB:pg * C.T + (qb + 1) * C.QB],
                        ps_ctx[1][0:64, 0:C.QB], sb["bc_sb"][64:128, 0:C.QB])
                    kb.inc(ins, "sV")


def _p2(nc, kb, C, io, sb):
    b = dict(kb.c)
    NT4 = C.QB // 128
    NTI = C.NQB * NT4

    def v_z(zi):
        qb, oc = divmod(zi, C.DC)
        return b["sV"] + qb * (C.DC + NT4 * 3) + oc + 1

    def v_zall(qb):
        return b["sV"] + qb * (C.DC + NT4 * 3) + C.DC

    def v_ztm(ti):
        qb, tt = divmod(ti, NT4)
        return b["sV"] + qb * (C.DC + NT4 * 3) + C.DC + tt * 3 + 1

    def v_rinv(ti):
        return v_ztm(ti) + 1

    def v_ast(ti):
        return v_ztm(ti) + 2

    rec = {"o": [None] * (C.NQB * C.DC), "tp": [None] * NTI}
    xn_fm, lns, ctx = sb["xn_fm"], sb["lns"], sb["ctx"]

    with contextlib.ExitStack() as ps:
        E2 = ps.enter_context
        wo_sb = E2(nc.sbuf_tensor("wo_sb", [128, C.DC * C.D], BF16))
        gb_sb = E2(nc.sbuf_tensor("gb_sb", [128, 2 * C.D], F32))
        z_fm = E2(nc.sbuf_tensor("z_fm", [128, 2 * C.DC * C.QB], F32))
        z_tm = E2(nc.sbuf_tensor("z_tm", [128, 2 * C.D], F32))
        t1 = E2(nc.sbuf_tensor("t1", [128, C.D], F32))
        sqd2 = E2(nc.sbuf_tensor("sqd2", [128, C.D], BF16))
        ast = E2(nc.sbuf_tensor("ast", [128, 2 * C.D], BF16))
        po = [_psum(nc, ps, f"po{i}", 128, C.QB, F32) for i in range(2)]
        pz = [_psum(nc, ps, f"pz{i}", 128, C.D, F32) for i in range(2)]
        blk = E2(nc.Block("p2"))

        @blk.gpsimd
        def _(g):
            kb.inc(g.dma_start(
                out=bass.AP(wo_sb, 0, [[C.DC * C.D, 128], [C.D, C.DC], [1, C.D]]),
                in_=io["wo"].rearrange("(c p) f -> p c f", p=128)), "sg", 16)

        @blk.sync
        def _(s):
            kb.inc(s.dma_start(out=gb_sb[:, :], in_=io["gbbc"]), "sx", 16)
            for ti in range(NTI):
                kb.w(s, "sV", v_ast(ti))
                kb.inc(s.dma_start(out=io["attn_scr"][ti * 128:(ti + 1) * 128, :],
                                   in_=ast[:, (ti % 2) * C.D:(ti % 2 + 1) * C.D]), "sx", 16)

        @blk.tensor
        def _(t):
            kb.w(t, "sg", b["sg"] + 16)
            for qb in range(C.NQB):
                for oc in range(C.DC):
                    zi = qb * C.DC + oc
                    if zi >= 2:
                        kb.w(t, "sV", v_z(zi - 2))
                    for cc in range(C.DC):
                        ins = t.matmul(po[zi % 2][:, 0:C.QB],
                                       wo_sb[:, cc * C.D + oc * 128:cc * C.D + (oc + 1) * 128],
                                       ctx[:, cc * C.T + qb * C.QB:cc * C.T + (qb + 1) * C.QB],
                                       start=(cc == 0), stop=(cc == C.DC - 1))
                    rec["o"][zi] = kb.inc(ins, "sT")
                for tt in range(NT4):
                    ti = qb * NT4 + tt
                    kb.w(t, "sV", v_zall(qb))
                    if ti >= 2:
                        kb.w(t, "sV", v_ztm(ti - 2))
                    for oc in range(C.DC):
                        base = (qb % 2) * C.DC * C.QB + oc * C.QB + tt * 128
                        ins = t.transpose(pz[ti % 2][:, oc * 128:(oc + 1) * 128],
                                          z_fm[:, base:base + 128], sb["idf"][:, :])
                    rec["tp"][ti] = kb.inc(ins, "sT")

        @blk.scalar
        def _(s):
            for ti in range(NTI):
                sl = ti % 2
                kb.w(s, "sV", v_ztm(ti))
                ins = s.activation(out=sqd2[:, :], in_=z_tm[:, sl * C.D:(sl + 1) * C.D],
                                   func=AF.Square,
                                   accum_out=lns[:, LNS_SSQ0 + sl:LNS_SSQ0 + sl + 1])
                kb.inc(ins, "sS")
                kb.w(s, "sV", v_rinv(ti))
                ins = s.activation(out=lns[:, LNS_R0:LNS_R0 + 1],
                                   in_=lns[:, LNS_RINV:LNS_RINV + 1], func=AF.Sqrt)
                kb.inc(ins, "sS")

        @blk.vector
        def _(v):
            kb.w(v, "sx", b["sx"] + 16)
            for qb in range(C.NQB):
                for oc in range(C.DC):
                    zi = qb * C.DC + oc
                    kb.w(v, "sT", rec["o"][zi])
                    if qb >= 2:
                        kb.w(v, "sT", rec["tp"][(qb - 2) * NT4 + NT4 - 1])
                    base = (qb % 2) * C.DC * C.QB + oc * C.QB
                    ins = v.scalar_tensor_tensor(
                        out=z_fm[:, base:base + C.QB],
                        in0=po[zi % 2][:, 0:C.QB], scalar=sb["bot"][:, oc:oc + 1],
                        in1=xn_fm[:, oc * C.T + qb * C.QB:oc * C.T + (qb + 1) * C.QB],
                        op0=ALU.add, op1=ALU.add)
                    kb.inc(ins, "sV")
                for tt in range(NT4):
                    ti = qb * NT4 + tt
                    sl = ti % 2
                    kb.w(v, "sT", rec["tp"][ti])
                    ins = v.tensor_copy(out=z_tm[:, sl * C.D:(sl + 1) * C.D],
                                        in_=pz[sl][:, 0:C.D])
                    n = kb.inc(ins, "sV")
                    kb.w(v, "sV", n)
                    kb.w(v, "sS", b["sS"] + 2 * ti + 1)
                    ins = _ln_stats(v, kb, lns, LNS_SSQ0 + sl,
                                    z_tm[:, sl * C.D:(sl + 1) * C.D], C.D)
                    kb.inc(ins, "sV")
                    kb.w(v, "sS", b["sS"] + 2 * ti + 2)
                    _ln_newton(v, kb, lns)
                    ins = v.scalar_tensor_tensor(out=t1[:, :], in0=z_tm[:, sl * C.D:(sl + 1) * C.D],
                                                 scalar=lns[:, LNS_MU:LNS_MU + 1],
                                                 in1=gb_sb[:, 0:C.D],
                                                 op0=ALU.subtract, op1=ALU.mult)
                    kb.vsync(v, ins)
                    if ti >= 2:
                        kb.w(v, "sx", b["sx"] + 16 * (1 + (ti - 2) + 1))
                    ins = v.scalar_tensor_tensor(out=ast[:, sl * C.D:(sl + 1) * C.D],
                                                 in0=t1[:, :],
                                                 scalar=lns[:, LNS_RSTD:LNS_RSTD + 1],
                                                 in1=gb_sb[:, C.D:2 * C.D],
                                                 op0=ALU.mult, op1=ALU.add)
                    kb.inc(ins, "sV")


def _p3(nc, kb, C, io, sb, p3st):
    b = dict(kb.c)
    PER_T = 1 + 2 * C.HC + C.DC
    PER_V = 1 + 2 * C.HC + C.DC

    def t_bc(esl):
        return b["sT"] + esl * PER_T + 1

    def t_u(esl, hc):
        return b["sT"] + esl * PER_T + 1 + 2 * hc + 1

    def t_v(esl, hc):
        return b["sT"] + esl * PER_T + 1 + 2 * hc + 2

    def t_y(esl, oc):
        return b["sT"] + esl * PER_T + 1 + 2 * C.HC + oc + 1

    def v_wbc(esl):
        return b["sV"] + esl * PER_V + 1

    def v_tt1(esl, hc):
        return b["sV"] + esl * PER_V + 1 + 2 * hc + 1

    def v_tt2(esl, hc):
        return b["sV"] + esl * PER_V + 1 + 2 * hc + 2

    def v_acc(esl, oc):
        return b["sV"] + esl * PER_V + 1 + 2 * C.HC + oc + 1

    def s_silu(esl, hc):
        return b["sS"] + esl * C.HC + hc + 1

    xn_fm, wT8, acc = sb["xn_fm"], sb["wT8"], sb["acc"]
    rg = {"w12": {}, "w1s": {}, "W3": {}}

    with contextlib.ExitStack() as ps:
        E3 = ps.enter_context
        W3_sb = E3(nc.sbuf_tensor("W3_sb", [128, 3 * C.HC * 128], BF16))
        h_sb = E3(nc.sbuf_tensor("h_sb", [128, C.HC * C.SL], BF16))
        su_sb = E3(nc.sbuf_tensor("su_sb", [128, 2 * C.SL], BF16))
        tb_sb = E3(nc.sbuf_tensor("tb_sb", [128, C.SL], BF16))
        wbc_sb = E3(nc.sbuf_tensor("wbc_sb", [128, C.SL], BF16))
        w1s = E3(nc.sbuf_tensor("w1s", [1, 2 * C.SL], BF16))
        pu = [_psum(nc, ps, f"pu{i}", 128, C.SL, F32) for i in range(2)]
        pvv = [_psum(nc, ps, f"pv3{i}", 128, C.SL, F32) for i in range(2)]
        py = [_psum(nc, ps, f"py{i}", 128, C.SL, F32) for i in range(3)]
        pb = _psum(nc, ps, "pb", 128, C.SL, F32)
        blk = E3(nc.Block("p3"))

        NW3 = 3
        H2 = C.HE // 2

        @blk.gpsimd
        def _(g):
            def load_w1s(esl):
                e, sl = divmod(esl, C.NSL)
                if esl >= 2:
                    kb.w(g, "sT", t_bc(esl - 2))
                n = kb.inc(g.dma_start(out=w1s[0:1, (esl % 2) * C.SL:(esl % 2 + 1) * C.SL],
                                       in_=wT8[e:e + 1, sl * C.SL:(sl + 1) * C.SL]),
                           "sg", 16)
                rg["w1s"][esl] = n

            def load_W3(esl, oc):
                e = esl // C.NSL
                w = esl * C.DC + oc
                if w >= NW3:
                    esl2, oc2 = divmod(w - NW3, C.DC)
                    kb.w(g, "sT", t_y(esl2, oc2))
                n = kb.inc(g.dma_start(
                    out=bass.AP(W3_sb, (w % NW3) * C.HC * 128,
                                [[NW3 * C.HC * 128, 128], [128, C.HC], [1, 128]]),
                    in_=bass.AP(io["W3"].tensor, e * C.HE * C.D + oc * 128,
                                [[C.D, 128], [128 * C.D, C.HC], [1, 128]])), "sg", 16)
                rg["W3"][w] = n

            for e in range(C.NE):
                load_w1s(e * C.NSL)
                for oc in range(min(NW3, C.DC)):
                    load_W3(e * C.NSL, oc)
                for half in range(2):
                    if e > 0:
                        kb.w(g, "sT", t_v((e - 1) * C.NSL + C.NSL - 1,
                                          (half + 1) * (C.HC // 2) - 1))
                    for wn, wsb in (("W1", "W1s"), ("W2", "W2s")):
                        n = kb.inc(g.dma_start(
                            out=bass.AP(sb[wsb], half * H2,
                                        [[C.DC * C.HE, 128], [C.HE, C.DC], [1, H2]]),
                            in_=io[wn][e][:, half * H2:(half + 1) * H2
                                          ].rearrange("(c p) h -> p c h", p=128)),
                            "sg", 16)
                    rg["w12"][(e, half)] = n
                for oc in range(min(NW3, C.DC), C.DC):
                    load_W3(e * C.NSL, oc)
                for sl in range(1, C.NSL):
                    esl = e * C.NSL + sl
                    load_w1s(esl)
                    for oc in range(C.DC):
                        load_W3(esl, oc)

        @blk.tensor
        def _(t):
            for e in range(C.NE):
                for sl in range(C.NSL):
                    esl = e * C.NSL + sl
                    kb.w(t, "sg", rg["w1s"][esl])
                    if esl >= 1:
                        kb.w(t, "sV", v_wbc(esl - 1))
                    ins = t.matmul(pb[:, 0:C.SL], sb["ones"][:, :],
                                   w1s[0:1, (esl % 2) * C.SL:(esl % 2 + 1) * C.SL],
                                   start=True, stop=True)
                    kb.inc(ins, "sT")
                    for hc in range(C.HC):
                        if sl == 0 and hc % (C.HC // 2) == 0:
                            kb.w(t, "sg", rg["w12"][(e, hc // (C.HC // 2))])
                        if hc >= 2:
                            kb.w(t, "sS", s_silu(esl, hc - 2))
                        elif esl >= 1:
                            kb.w(t, "sS", s_silu(esl - 1, C.HC - 2 + hc))
                        for dc in range(C.DC):
                            ins = t.matmul(pu[hc % 2][:, 0:C.SL],
                                           sb["W1s"][:, dc * C.HE + hc * 128:dc * C.HE + (hc + 1) * 128],
                                           xn_fm[:, dc * C.T + sl * C.SL:dc * C.T + (sl + 1) * C.SL],
                                           start=(dc == 0), stop=(dc == C.DC - 1))
                        kb.inc(ins, "sT")
                        if hc >= 2:
                            kb.w(t, "sV", v_tt1(esl, hc - 2))
                        elif esl >= 1:
                            kb.w(t, "sV", v_tt1(esl - 1, C.HC - 2 + hc))
                        for dc in range(C.DC):
                            ins = t.matmul(pvv[hc % 2][:, 0:C.SL],
                                           sb["W2s"][:, dc * C.HE + hc * 128:dc * C.HE + (hc + 1) * 128],
                                           xn_fm[:, dc * C.T + sl * C.SL:dc * C.T + (sl + 1) * C.SL],
                                           start=(dc == 0), stop=(dc == C.DC - 1))
                        kb.inc(ins, "sT")
                    kb.w(t, "sV", v_tt2(esl, C.HC - 1))
                    for oc in range(C.DC):
                        w = esl * C.DC + oc
                        kb.w(t, "sg", rg["W3"][w])
                        if w >= 3:
                            kb.w(t, "sV", v_acc(*divmod(w - 3, C.DC)))
                        for hc in range(C.HC):
                            ins = t.matmul(py[w % 3][:, 0:C.SL],
                                           bass.AP(W3_sb, (w % NW3) * C.HC * 128 + hc * 128,
                                                   [[NW3 * C.HC * 128, 128], [1, 128]]),
                                           h_sb[:, hc * C.SL:(hc + 1) * C.SL],
                                           start=(hc == 0), stop=(hc == C.HC - 1))
                        kb.inc(ins, "sT")
                    if esl == C.NE * C.NSL - 1:
                        ins = t.matmul(pb[:, 0:C.SL], sb["ones"][:, :],
                                       w1s[0:1, (esl % 2) * C.SL:(esl % 2 + 1) * C.SL],
                                       start=True, stop=True)
                        kb.inc(ins, "sT")

        @blk.scalar
        def _(s):
            for esl in range(C.NE * C.NSL):
                for hc in range(C.HC):
                    kb.w(s, "sT", t_u(esl, hc))
                    if hc >= 2:
                        kb.w(s, "sV", v_tt1(esl, hc - 2))
                    elif esl >= 1:
                        kb.w(s, "sV", v_tt1(esl - 1, C.HC - 2 + hc))
                    ins = s.activation(out=su_sb[:, (hc % 2) * C.SL:(hc % 2 + 1) * C.SL],
                                       in_=pu[hc % 2][:, 0:C.SL], func=AF.Silu)
                    kb.inc(ins, "sS")

        @blk.vector
        def _(v):
            for e in range(C.NE):
                for sl in range(C.NSL):
                    esl = e * C.NSL + sl
                    kb.w(v, "sT", t_bc(esl))
                    ins = v.tensor_copy(out=wbc_sb[:, :], in_=pb[:, 0:C.SL])
                    n = kb.inc(ins, "sV")
                    kb.w(v, "sV", n)
                    for hc in range(C.HC):
                        kb.w(v, "sT", t_v(esl, hc))
                        kb.w(v, "sS", s_silu(esl, hc))
                        ins = v.tensor_mul(tb_sb[:, :], pvv[hc % 2][:, 0:C.SL],
                                           su_sb[:, (hc % 2) * C.SL:(hc % 2 + 1) * C.SL])
                        n = kb.inc(ins, "sV")
                        kb.w(v, "sV", n)
                        if hc == 0 and esl >= 1:
                            kb.w(v, "sT", t_y(esl - 1, C.DC - 1))
                        ins = v.tensor_mul(h_sb[:, hc * C.SL:(hc + 1) * C.SL],
                                           tb_sb[:, :], wbc_sb[:, :])
                        if hc == C.HC - 1:
                            kb.vsync(v, ins)
                            v.sem_inc(kb.sem["sV"], 1)
                            kb.c["sV"] += 1
                        else:
                            kb.inc(ins, "sV")
                    for oc in range(C.DC):
                        if oc < C.DC - 1:
                            kb.w(v, "sT", t_y(esl, oc + 1))
                        elif esl < C.NE * C.NSL - 1:
                            kb.w(v, "sT", t_bc(esl + 1))
                        else:
                            kb.w(v, "sT", t_y(esl, oc) + 1)
                        dst = acc[:, oc * C.T + sl * C.SL:oc * C.T + (sl + 1) * C.SL]
                        w = esl * C.DC + oc
                        if e == 0:
                            ins = v.tensor_copy(out=dst, in_=py[w % 3][:, 0:C.SL])
                        else:
                            ins = v.tensor_add(dst, dst, py[w % 3][:, 0:C.SL])
                        kb.inc(ins, "sV")


def _p4(nc, kb, C, io, sb):
    b = dict(kb.c)
    NT = C.NTT
    rec = {"tp": [None] * NT}
    acc = sb["acc"]

    with contextlib.ExitStack() as ps:
        E4 = ps.enter_context
        xin4 = E4(nc.sbuf_tensor("xin4", [128, 2 * C.D], F32))
        atn4 = E4(nc.sbuf_tensor("atn4", [128, 2 * C.D], BF16))
        tmp4 = E4(nc.sbuf_tensor("tmp4", [128, C.D], F32))
        out4 = E4(nc.sbuf_tensor("out4", [128, 2 * C.D], F32))
        pz4 = [_psum(nc, ps, f"pz4{i}", 128, C.D, F32) for i in range(2)]
        blk = E4(nc.Block("p4"))

        @blk.sync
        def _(s):
            for i in range(NT + 1):
                if i < NT:
                    sl = i % 2
                    if i >= 2:
                        kb.w(s, "sV", b["sV"] + (i - 2) + 1)
                    kb.inc(s.dma_start(out=xin4[:, sl * C.D:(sl + 1) * C.D],
                                       in_=io["x_seq"][i * 128:(i + 1) * 128, :]), "sx", 16)
                    kb.inc(s.dma_start(out=atn4[:, sl * C.D:(sl + 1) * C.D],
                                       in_=io["attn_scr"][i * 128:(i + 1) * 128, :]), "sx", 16)
                if i >= 1:
                    j = i - 1
                    kb.w(s, "sV", b["sV"] + j + 1)
                    kb.inc(s.dma_start(out=io["out"][j * 128:(j + 1) * 128, :],
                                       in_=out4[:, (j % 2) * C.D:(j % 2 + 1) * C.D]), "sx", 16)
            s.wait_ge(kb.sem["sx"], kb.c["sx"])

        @blk.tensor
        def _(t):
            for i in range(NT):
                sl = i % 2
                if i >= 2:
                    kb.w(t, "sV", b["sV"] + (i - 2) + 1)
                for oc in range(C.DC):
                    ins = t.transpose(pz4[sl][:, oc * 128:(oc + 1) * 128],
                                      acc[:, oc * C.T + i * 128:oc * C.T + (i + 1) * 128],
                                      sb["idf"][:, :])
                rec["tp"][i] = kb.inc(ins, "sT")

        @blk.vector
        def _(v):
            for i in range(NT):
                sl = i % 2
                kb.w(v, "sT", rec["tp"][i])
                kb.w(v, "sx", b["sx"] + 16 * (2 if i == 0 else 3 * i + 1))
                if i >= 2:
                    kb.w(v, "sx", b["sx"] + 16 * (3 * (i - 2) + 5))
                ins = v.tensor_add(tmp4[:, :], pz4[sl][:, 0:C.D],
                                    xin4[:, sl * C.D:(sl + 1) * C.D])
                kb.vsync(v, ins)
                ins = v.tensor_add(out4[:, sl * C.D:(sl + 1) * C.D], tmp4[:, :],
                                   atn4[:, sl * C.D:(sl + 1) * C.D])
                kb.inc(ins, "sV")


def _aux_inputs(C: Cfg):
    bf = ml_dtypes.bfloat16
    ebc = np.zeros((33, 128), bf)
    ebc[0, 0:64] = 1
    ebc[32, 64:128] = 1
    return {
        "identbf": np.eye(128, dtype=bf),
        "identf32": np.eye(128, dtype=np.float32),
        "ones1": np.ones((1, 128), bf),
        "ebc": ebc,
    }


def _layout_inputs(C: Cfg, inp):
    f = lambda k: np.ascontiguousarray(np.asarray(inp[k], np.float32))
    com = {
        "wq": f("L0_attn_wq"), "wk": f("L0_attn_wk"), "wv": f("L0_attn_wv"),
        "wo": f("L0_attn_wo"), "Wg": f("L0_ffn_Wg"),
        "W1": f("L0_ffn_W1"), "W2": f("L0_ffn_W2"), "W3": f("L0_ffn_W3"),
        "bqt": np.ascontiguousarray(f("L0_attn_bq").reshape(C.DC, 128).T),
        "bkt": np.ascontiguousarray(f("L0_attn_bk").reshape(C.DC, 128).T),
        "bot": np.ascontiguousarray(f("L0_attn_bo").reshape(C.DC, 128).T),
        "gbbc": np.ascontiguousarray(np.concatenate(
            [np.broadcast_to(f("L0_attn_gamma")[None, :], (128, C.D)),
             np.broadcast_to(f("L0_attn_beta")[None, :], (128, C.D))], axis=1)),
        "bvbc": np.ascontiguousarray(np.broadcast_to(f("L0_attn_bv")[None, :], (128, C.D))),
    }
    com.update(_aux_inputs(C))
    return com


def kernel(**inputs):
    C = Cfg()
    nc = build_program(C)
    x = np.asarray(inputs["L0_attn_x"], np.float32)
    B = x.shape[0]
    com = _layout_inputs(C, inputs)
    in_maps = []
    for c in range(2 * B):
        bi, hp = divmod(c, 2)
        xs = x[bi]
        if hp == 1:
            xs = np.concatenate([xs[C.T:], xs[:C.T]], axis=0)
        m = dict(com)
        m["x_seq"] = np.ascontiguousarray(xs)
        in_maps.append(m)
    res = run_bass_kernel_spmd(nc, in_maps, core_ids=list(range(2 * B)))
    out = np.empty((B, C.S, C.D), np.float32)
    for c in range(2 * B):
        bi, hp = divmod(c, 2)
        out[bi, hp * C.T:(hp + 1) * C.T] = res.results[c]["out"]
    return out


# version 16 (final)
# speedup vs baseline: 1.0531x; 1.0202x over previous
import contextlib

import numpy as np
import ml_dtypes

import concourse.bass as bass
import concourse.mybir as mybir
from concourse.bass_utils import run_bass_kernel_spmd

F32 = mybir.dt.float32
BF16 = mybir.dt.bfloat16
AF = mybir.ActivationFunctionType
ALU = mybir.AluOpType
AX = mybir.AxisListType

LN_EPS = 1e-5

(LNS_SSUM, LNS_SSQ0, LNS_SSQ1, LNS_MU, LNS_MUSQ, LNS_VEPS, LNS_RINV, LNS_R0,
 LNS_R2, LNS_RSTD, LNS_D01, LNS_D10, LNS_P0, LNS_P1, LNS_V0, LNS_V1) = range(16)


class Cfg:
    def __init__(self, S=4096, D=1024, NH=16, HE=2048, NE=8, NGRP=4, dbg=False,
                 stop_after=None, dbg_taps=None):
        self.S, self.D, self.NH, self.HE, self.NE, self.NGRP, self.dbg = \
            S, D, NH, HE, NE, NGRP, dbg
        self.stop_after = stop_after
        self.dbg_taps = dbg_taps if dbg_taps is not None else {'p0', 'p1p', 'p1d', 'p3d'}
        self.HD = 64
        assert D == NH * self.HD
        self.T = S // 2
        self.DC = D // 128
        self.NKC = S // 128
        self.NTT = self.T // 128
        self.QB = min(512, self.T)
        self.NQB = self.T // self.QB
        self.KBLK = 512
        self.NKB = S // self.KBLK
        self.NKBo = self.T // self.KBLK
        self.HG = NH // NGRP
        self.PHG = self.HG // 2
        self.FH = self.HG * 64
        self.HC = HE // 128
        self.SL = min(512, self.T)
        self.NSL = self.T // self.SL
        self.G = 2
        self.NG = self.NKC // self.G
        assert D % 128 == 0 and self.T % 128 == 0 and S % 512 == 0
        assert self.HG % 2 == 0 and self.NKC % self.G == 0 and self.FH <= 512
        assert self.NTT <= 16


class KB:
    SEMS = ["sx", "sg", "sT", "sS", "sV", "v2", "s2"]

    def __init__(self, nc, st):
        self.nc = nc
        self.sem = {n: st.enter_context(nc.semaphore(n)) for n in self.SEMS}
        self.c = {n: 0 for n in self.SEMS}

    def inc(self, instr, n, k=1):
        instr.then_inc(self.sem[n], k)
        self.c[n] += k
        return self.c[n]

    def w(self, eng, n, v):
        if v > 0:
            eng.wait_ge(self.sem[n], v)

    def vsync(self, eng, ins, sem="v2"):
        n = self.inc(ins, sem)
        eng.wait_ge(self.sem[sem], n)


def _psum(nc, st, name, p, f, dt):
    per = 512 if dt == F32 else 1024
    fpad = ((f + per - 1) // per) * per
    t = st.enter_context(nc.psum_tensor(name, [p, fpad], dt))
    return t


def _ln_stats(v, kb, lns, ssq_c, x_ap, n_feat):
    ins = v.reduce_sum(lns[:, LNS_SSUM:LNS_SSUM + 1], x_ap, axis=AX.X)
    kb.vsync(v, ins)
    ins = v.tensor_scalar_mul(out=lns[:, LNS_MU:LNS_MU + 1],
                              in0=lns[:, LNS_SSUM:LNS_SSUM + 1], scalar1=1.0 / n_feat)
    kb.vsync(v, ins)
    ins = v.tensor_mul(lns[:, LNS_MUSQ:LNS_MUSQ + 1], lns[:, LNS_MU:LNS_MU + 1],
                       lns[:, LNS_MU:LNS_MU + 1])
    kb.vsync(v, ins)
    ins = v.scalar_tensor_tensor(out=lns[:, LNS_VEPS:LNS_VEPS + 1],
                                 in0=lns[:, ssq_c:ssq_c + 1], scalar=1.0 / n_feat,
                                 in1=lns[:, LNS_MUSQ:LNS_MUSQ + 1],
                                 op0=ALU.mult, op1=ALU.subtract)
    kb.vsync(v, ins)
    ins = v.tensor_scalar_add(out=lns[:, LNS_VEPS:LNS_VEPS + 1],
                              in0=lns[:, LNS_VEPS:LNS_VEPS + 1], scalar1=LN_EPS)
    kb.vsync(v, ins)
    return v.reciprocal(lns[:, LNS_RINV:LNS_RINV + 1], lns[:, LNS_VEPS:LNS_VEPS + 1])


def _ln_newton(v, kb, lns):
    ins = v.tensor_mul(lns[:, LNS_R2:LNS_R2 + 1], lns[:, LNS_R0:LNS_R0 + 1],
                       lns[:, LNS_R0:LNS_R0 + 1])
    kb.vsync(v, ins)
    ins = v.tensor_single_scalar(out=lns[:, LNS_R2:LNS_R2 + 1],
                                 in_=lns[:, LNS_R2:LNS_R2 + 1],
                                 scalar=lns[:, LNS_VEPS:LNS_VEPS + 1], op=ALU.mult)
    kb.vsync(v, ins)
    ins = v.tensor_scalar(out=lns[:, LNS_R2:LNS_R2 + 1], in0=lns[:, LNS_R2:LNS_R2 + 1],
                          scalar1=-0.5, scalar2=1.5, op0=ALU.mult, op1=ALU.add)
    kb.vsync(v, ins)
    ins = v.tensor_mul(lns[:, LNS_RSTD:LNS_RSTD + 1], lns[:, LNS_R0:LNS_R0 + 1],
                       lns[:, LNS_R2:LNS_R2 + 1])
    kb.vsync(v, ins)


RACE_DETECT = False


def build_program(C: Cfg):
    nc = bass.Bass("TRN2", target_bir_lowering=False, debug=False,
                   detect_race_conditions=RACE_DETECT)
    dt = nc.dram_tensor
    io = {}
    io["x_seq"] = dt("x_seq", [C.S, C.D], F32, kind="ExternalInput").ap()
    for n in ("wq", "wk", "wv", "wo"):
        io[n] = dt(n, [C.D, C.D], F32, kind="ExternalInput").ap()
    io["Wg"] = dt("Wg", [C.D, C.NE], F32, kind="ExternalInput").ap()
    io["W1"] = dt("W1", [C.NE, C.D, C.HE], F32, kind="ExternalInput").ap()
    io["W2"] = dt("W2", [C.NE, C.D, C.HE], F32, kind="ExternalInput").ap()
    io["W3"] = dt("W3", [C.NE, C.HE, C.D], F32, kind="ExternalInput").ap()
    io["identbf"] = dt("identbf", [128, 128], BF16, kind="ExternalInput").ap()
    io["identf32"] = dt("identf32", [128, 128], F32, kind="ExternalInput").ap()
    io["ones1"] = dt("ones1", [1, 128], BF16, kind="ExternalInput").ap()
    io["ebc"] = dt("ebc", [33, 128], BF16, kind="ExternalInput").ap()
    io["bqt"] = dt("bqt", [128, C.DC], F32, kind="ExternalInput").ap()
    io["bkt"] = dt("bkt", [128, C.DC], F32, kind="ExternalInput").ap()
    io["bot"] = dt("bot", [128, C.DC], F32, kind="ExternalInput").ap()
    io["gbbc"] = dt("gbbc", [128, 2 * C.D], F32, kind="ExternalInput").ap()
    io["bvbc"] = dt("bvbc", [128, C.D], F32, kind="ExternalInput").ap()
    io["attn_scr"] = dt("attn_scr", [C.T, C.D], BF16, kind="Internal").ap()
    io["out"] = dt("out", [C.T, C.D], F32, kind="ExternalOutput").ap()
    if C.dbg:
        io["dbg_xnfm"] = dt("dbg_xnfm", [128, C.DC * C.T], BF16, kind="ExternalOutput").ap()
        io["dbg_lns"] = dt("dbg_lns", [128, 16], F32, kind="ExternalOutput").ap()
        io["dbg_xntm"] = dt("dbg_xntm", [128, 2 * C.D], BF16, kind="ExternalOutput").ap()
        io["dbg_xin"] = dt("dbg_xin", [128, 2 * C.D], F32, kind="ExternalOutput").ap()
        io["dbg_wT8"] = dt("dbg_wT8", [8, C.T], F32, kind="ExternalOutput").ap()
        io["dbg_K"] = dt("dbg_K", [C.NGRP, 128, C.PHG * C.S], BF16, kind="ExternalOutput").ap()
        io["dbg_V"] = dt("dbg_V", [C.NGRP, 128, C.NKC * C.HG * 65], BF16, kind="ExternalOutput").ap()
        io["dbg_Q"] = dt("dbg_Q", [C.NGRP, 128, C.PHG * C.T], BF16, kind="ExternalOutput").ap()
        io["dbg_ctx"] = dt("dbg_ctx", [128, (C.NH // 2) * C.T], BF16, kind="ExternalOutput").ap()
        io["dbg_acc"] = dt("dbg_acc", [128, C.DC * C.T], F32, kind="ExternalOutput").ap()

    with contextlib.ExitStack() as st:
        kb = KB(nc, st)
        E = st.enter_context
        sb = {}
        sb["xn_fm"] = E(nc.sbuf_tensor("xn_fm", [128, C.DC * C.T], BF16))
        sb["wT8"] = E(nc.sbuf_tensor("wT8", [8, C.T], F32))
        sb["idb"] = E(nc.sbuf_tensor("idb", [128, 128], BF16))
        sb["idf"] = E(nc.sbuf_tensor("idf", [128, 128], F32))
        sb["ones"] = E(nc.sbuf_tensor("ones", [1, 128], BF16))
        sb["ebc"] = E(nc.sbuf_tensor("ebc_s", [33, 128], BF16))
        sb["bqt"] = E(nc.sbuf_tensor("bqt_s", [128, C.DC], F32))
        sb["bkt"] = E(nc.sbuf_tensor("bkt_s", [128, C.DC], F32))
        sb["bot"] = E(nc.sbuf_tensor("bot_s", [128, C.DC], F32))
        sb["Wg"] = E(nc.sbuf_tensor("Wgs", [128, C.DC * C.NE], BF16))
        sb["lns"] = E(nc.sbuf_tensor("lns", [128, 16], F32))
        sb["w_tm"] = E(nc.sbuf_tensor("w_tm", [128, C.NTT * C.NE], F32))
        sb["gsc"] = E(nc.sbuf_tensor("gsc", [128, 4 * C.NE], F32))
        sb["z8"] = E(nc.sbuf_tensor("z8", [128, 8], F32))
        sb["n8"] = E(nc.sbuf_tensor("n8", [128, 8], F32))

        with contextlib.ExitStack() as p02:
            sb["xn_oth"] = p02.enter_context(
                nc.sbuf_tensor("xn_oth", [128, C.DC * C.T], BF16))
            _p0(nc, kb, C, io, sb, st)

            with contextlib.ExitStack() as p1s:
                sb["ctx"] = p1s.enter_context(
                    nc.sbuf_tensor("ctx", [128, (C.NH // 2) * C.T], BF16))
                with contextlib.ExitStack() as p1o:
                    E1 = p1o.enter_context
                    sb["wq_g"] = E1(nc.sbuf_tensor("wq_g", [128, C.DC * C.FH], BF16))
                    sb["wk_g"] = E1(nc.sbuf_tensor("wk_g", [128, C.DC * C.FH], BF16))
                    sb["wv_g"] = E1(nc.sbuf_tensor("wv_g", [128, C.DC * C.FH], BF16))
                    sb["K"] = E1(nc.sbuf_tensor("Ksb", [128, C.PHG * C.S], BF16))
                    sb["V"] = E1(nc.sbuf_tensor("Vsb", [128, C.NKC * C.HG * 65], BF16))
                    sb["Q"] = E1(nc.sbuf_tensor("Qsb", [128, C.PHG * C.T], BF16))
                    sb["p_sb"] = E1(nc.sbuf_tensor("p_sb", [128, 3 * C.G * C.QB], BF16))
                    sb["rZ2"] = E1(nc.sbuf_tensor("rZ2", [33, C.QB], BF16))
                    sb["bc_sb"] = E1(nc.sbuf_tensor("bc_sb", [128, C.QB], BF16))
                    sb["bvbc"] = E1(nc.sbuf_tensor("bvbcs", [128, C.D], F32))
                    for grp in range(C.NGRP if C.stop_after != "p0" else 0):
                        _p1_proj(nc, kb, C, grp, io, sb)
                        if C.stop_after == "p1proj" and grp == 0:
                            break
                        _p1_attn(nc, kb, C, grp, io, sb)
                    if C.stop_after in ("p0", "p1proj", "p1"):
                        return nc
                    if C.dbg and 'p1d' in C.dbg_taps:
                        with nc.Block("p1d") as blk:
                            @blk.gpsimd
                            def _(g):
                                kb.inc(g.dma_start(out=io["dbg_ctx"], in_=sb["ctx"][:, :]), "sg", 16)
                                g.wait_ge(kb.sem["sg"], kb.c["sg"])
                _p2(nc, kb, C, io, sb)
        with contextlib.ExitStack() as p3s:
            E3 = p3s.enter_context
            sb["acc"] = E3(nc.sbuf_tensor("acc", [128, C.DC * C.T], F32))
            sb["W1s"] = E3(nc.sbuf_tensor("W1s", [128, C.DC * C.HE], BF16))
            sb["W2s"] = E3(nc.sbuf_tensor("W2s", [128, C.DC * C.HE], BF16))
            _p3(nc, kb, C, io, sb, p3s)
            if C.dbg and 'p3d' in C.dbg_taps:
                with nc.Block("p3d") as blk:
                    @blk.gpsimd
                    def _(g):
                        kb.inc(g.dma_start(out=io["dbg_acc"], in_=sb["acc"][:, :]), "sg", 16)
                        g.wait_ge(kb.sem["sg"], kb.c["sg"])
            _p4(nc, kb, C, io, sb)
    return nc


def _p0(nc, kb, C, io, sb, st_outer):
    b = dict(kb.c)
    NT = C.NTT
    NT2 = 2 * NT
    NCONST = 7
    rec = {"tp": [None] * NT2, "lg": None, "lgt": [None] * NT, "wt": [None] * NT}
    xn_fm, lns, gsc, w_tm, wT8 = sb["xn_fm"], sb["lns"], sb["gsc"], sb["w_tm"], sb["wT8"]
    BV = 3 * NT + 2 + 2 * NT + 1

    def v_rinv(i):
        return b["sV"] + (3 * i + 1 if i < NT else BV + 3 * (i - NT) + 1)

    def v_xntm(i):
        return v_rinv(i) + 1

    def v_evac(i):
        return v_rinv(i) + 2

    def s_sq(i):
        return b["sS"] + (2 * i + 1 if i < NT else 3 * NT + 2 * (i - NT) + 1)

    def s_r0(i):
        return s_sq(i) + 1

    with contextlib.ExitStack() as ps:
        E0 = ps.enter_context
        xin = E0(nc.sbuf_tensor("xin", [128, 2 * C.D], F32))
        xntm = E0(nc.sbuf_tensor("xntm", [128, 2 * C.D], BF16))
        sqd = E0(nc.sbuf_tensor("sqd", [128, C.D], BF16))
        lgf = E0(nc.sbuf_tensor("lgf", [8, C.T], F32))
        lgtm = E0(nc.sbuf_tensor("lgtm", [128, C.NTT * C.NE], F32))
        pst = [_psum(nc, ps, f"pst{i}", 128, C.D, BF16) for i in range(2)]
        ps_g = _psum(nc, ps, "ps_g", 8, C.T, F32)
        ps_t = _psum(nc, ps, "ps_t", 128, NT * C.NE, F32)
        blk = E0(nc.Block("p0"))

        def ln_dst(i):
            if i < NT:
                return bass.AP(xn_fm, i * 128, [[C.DC * C.T, 128], [C.T, C.DC], [1, 128]])
            return bass.AP(sb["xn_oth"], (i - NT) * 128,
                           [[C.DC * C.T, 128], [C.T, C.DC], [1, 128]])

        @blk.gpsimd
        def _(g):
            kb.inc(g.dma_start(
                out=bass.AP(sb["Wg"], 0, [[C.DC * C.NE, 128], [C.NE, C.DC], [1, C.NE]]),
                in_=io["Wg"].rearrange("(c p) e -> p c e", p=128)), "sg", 16)

        @blk.sync
        def _(s):
            for name in ("identbf", "identf32", "ones1", "ebc", "bqt", "bkt", "bot"):
                tgt = {"identbf": "idb", "identf32": "idf", "ones1": "ones",
                       "ebc": "ebc", "bqt": "bqt", "bkt": "bkt", "bot": "bot"}[name]
                kb.inc(s.dma_start(out=sb[tgt][:, :], in_=io[name]), "sx", 16)
            for i in range(NT2):
                if i >= 2:
                    kb.w(s, "sV", v_xntm(i - 2))
                kb.inc(s.dma_start(out=xin[:, (i % 2) * C.D:(i % 2 + 1) * C.D],
                                   in_=io["x_seq"][i * 128:(i + 1) * 128, :]), "sx", 16)

        def emit_ln_tensor(t, i):
            sl = i % 2
            kb.w(t, "sV", v_xntm(i))
            if i >= 2:
                kb.w(t, "sV", v_evac(i - 2))
            for dc in range(C.DC):
                ins = t.transpose(pst[sl][:, dc * 128:(dc + 1) * 128],
                                  xntm[:, sl * C.D + dc * 128:sl * C.D + (dc + 1) * 128],
                                  sb["idb"][:, :])
            rec["tp"][i] = kb.inc(ins, "sT")

        @blk.tensor
        def _(t):
            for i in range(NT):
                emit_ln_tensor(t, i)
            kb.w(t, "sg", b["sg"] + 16)
            kb.w(t, "sV", b["sV"] + 3 * NT)
            for qb in range(C.T // 512):
                for dc in range(C.DC):
                    ins = t.matmul(ps_g[0:C.NE, qb * 512:(qb + 1) * 512],
                                   bass.AP(sb["Wg"], dc * C.NE, [[C.DC * C.NE, 128], [1, C.NE]]),
                                   xn_fm[:, dc * C.T + qb * 512:dc * C.T + (qb + 1) * 512],
                                   start=(dc == 0), stop=(dc == C.DC - 1))
            rec["lg"] = kb.inc(ins, "sT")
            kb.w(t, "sV", b["sV"] + 3 * NT + 1)
            for i in range(NT):
                ins = t.transpose(ps_t[:, i * C.NE:i * C.NE + C.NE],
                                  lgf[0:8, i * 128:(i + 1) * 128], sb["idf"][0:8, 0:8])
                rec["lgt"][i] = kb.inc(ins, "sT")
            for i in range(NT):
                kb.w(t, "sV", b["sV"] + 3 * NT + 2 + 2 * i + 2)
                ins = t.transpose(ps_g[0:C.NE, i * 128:(i + 1) * 128],
                                  w_tm[:, i * C.NE:(i + 1) * C.NE], sb["idf"][:, :])
                rec["wt"][i] = kb.inc(ins, "sT")
            for i in range(NT, NT2):
                emit_ln_tensor(t, i)

        def emit_ln_scalar(s, i):
            sl = i % 2
            kb.w(s, "sx", b["sx"] + 16 * (NCONST + i + 1))
            if i >= 2:
                kb.w(s, "sV", v_rinv(i - 2))
            ins = s.activation(out=sqd[:, :], in_=xin[:, sl * C.D:(sl + 1) * C.D],
                               func=AF.Square,
                               accum_out=lns[:, LNS_SSQ0 + sl:LNS_SSQ0 + sl + 1])
            kb.inc(ins, "sS")
            kb.w(s, "sV", v_rinv(i))
            ins = s.activation(out=lns[:, LNS_R0:LNS_R0 + 1],
                               in_=lns[:, LNS_RINV:LNS_RINV + 1], func=AF.Sqrt)
            kb.inc(ins, "sS")

        @blk.scalar
        def _(s):
            for i in range(NT):
                emit_ln_scalar(s, i)
            for i in range(NT):
                kb.w(s, "sV", b["sV"] + 3 * NT + 2 + 2 * i + 1)
                s.activation(out=lns[:, LNS_P0:LNS_P0 + 1],
                             in_=lns[:, LNS_D01:LNS_D01 + 1], func=AF.Sigmoid)
                ins = s.activation(out=lns[:, LNS_P1:LNS_P1 + 1],
                                   in_=lns[:, LNS_D10:LNS_D10 + 1], func=AF.Sigmoid)
                kb.inc(ins, "sS")
            for i in range(NT, NT2):
                emit_ln_scalar(s, i)

        def emit_ln_vector(v, i):
            sl = i % 2
            kb.w(v, "sx", b["sx"] + 16 * (NCONST + i + 1))
            kb.w(v, "sS", s_sq(i))
            ins = _ln_stats(v, kb, lns, LNS_SSQ0 + sl, xin[:, sl * C.D:(sl + 1) * C.D], C.D)
            kb.inc(ins, "sV")
            kb.w(v, "sS", s_r0(i))
            _ln_newton(v, kb, lns)
            if i >= 2:
                kb.w(v, "sT", rec["tp"][i - 2])
            ins = v.tensor_scalar(out=xntm[:, sl * C.D:(sl + 1) * C.D],
                                  in0=xin[:, sl * C.D:(sl + 1) * C.D],
                                  scalar1=lns[:, LNS_MU:LNS_MU + 1],
                                  scalar2=lns[:, LNS_RSTD:LNS_RSTD + 1],
                                  op0=ALU.subtract, op1=ALU.mult)
            kb.inc(ins, "sV")
            kb.w(v, "sT", rec["tp"][i])
            ins = v.tensor_copy(out=ln_dst(i), in_=pst[sl][:, 0:C.D])
            kb.inc(ins, "sV")

        @blk.vector
        def _(v):
            for i in range(NT):
                emit_ln_vector(v, i)
            kb.w(v, "sT", rec["lg"])
            ins = v.tensor_copy(out=lgf[0:8, :], in_=ps_g[0:8, 0:C.T])
            kb.inc(ins, "sV")
            kb.w(v, "sT", rec["lgt"][NT - 1])
            ins = v.tensor_copy(out=lgtm[:, :], in_=ps_t[:, 0:NT * C.NE])
            kb.vsync(v, ins)
            ins = v.memset(sb["z8"][:, :], 0.0)
            kb.inc(ins, "sV")
            ins = v.memset(sb["z8"][:, :], 0.0)
            kb.vsync(v, ins)
            for i in range(NT):
                lg = lgtm[:, i * C.NE:(i + 1) * C.NE]
                ins = v.reduce_max(lns[:, LNS_V0:LNS_V0 + 1], lg, axis=AX.X)
                kb.vsync(v, ins)
                ins = v.tensor_scalar_add(out=sb["n8"][:, :], in0=sb["z8"][:, :],
                                          scalar1=lns[:, LNS_V0:LNS_V0 + 1])
                kb.vsync(v, ins)
                ins = v.match_replace(out=gsc[:, 0:C.NE], in_to_replace=sb["n8"][:, 0:8],
                                      in_values=lg, imm_value=-1e30)
                kb.vsync(v, ins)
                ins = v.reduce_max(lns[:, LNS_V1:LNS_V1 + 1], gsc[:, 0:C.NE], axis=AX.X)
                kb.vsync(v, ins)
                ins = v.tensor_scalar_add(out=sb["n8"][:, :], in0=sb["z8"][:, :],
                                          scalar1=lns[:, LNS_V1:LNS_V1 + 1])
                kb.vsync(v, ins)
                ins = v.match_replace(out=gsc[:, C.NE:2 * C.NE], in_to_replace=sb["n8"][:, 0:8],
                                      in_values=gsc[:, 0:C.NE], imm_value=-1e30)
                kb.vsync(v, ins)
                v.tensor_tensor(out=gsc[:, 2 * C.NE:3 * C.NE], in0=lg,
                                in1=gsc[:, 0:C.NE], op=ALU.not_equal)
                ins = v.tensor_tensor(out=gsc[:, 3 * C.NE:4 * C.NE], in0=gsc[:, 0:C.NE],
                                      in1=gsc[:, C.NE:2 * C.NE], op=ALU.not_equal)
                kb.vsync(v, ins)
                ins = v.tensor_sub(lns[:, LNS_D01:LNS_D01 + 1], lns[:, LNS_V0:LNS_V0 + 1],
                                   lns[:, LNS_V1:LNS_V1 + 1])
                kb.vsync(v, ins)
                ins = v.tensor_scalar_mul(out=lns[:, LNS_D10:LNS_D10 + 1],
                                          in0=lns[:, LNS_D01:LNS_D01 + 1], scalar1=-1.0)
                kb.inc(ins, "sV")
                kb.w(v, "sS", b["sS"] + 2 * NT + i + 1)
                ins = v.tensor_single_scalar(out=gsc[:, 2 * C.NE:3 * C.NE],
                                             in_=gsc[:, 2 * C.NE:3 * C.NE],
                                             scalar=lns[:, LNS_P0:LNS_P0 + 1], op=ALU.mult)
                kb.vsync(v, ins)
                ins = v.scalar_tensor_tensor(out=w_tm[:, i * C.NE:(i + 1) * C.NE],
                                             in0=gsc[:, 3 * C.NE:4 * C.NE],
                                             scalar=lns[:, LNS_P1:LNS_P1 + 1],
                                             in1=gsc[:, 2 * C.NE:3 * C.NE],
                                             op0=ALU.mult, op1=ALU.add)
                kb.inc(ins, "sV")
            kb.w(v, "sT", rec["wt"][NT - 1])
            ins = v.tensor_copy(out=wT8[0:8, :], in_=ps_g[0:8, 0:C.T])
            kb.inc(ins, "sV")
            for i in range(NT, NT2):
                emit_ln_vector(v, i)

        if C.dbg and 'p0' in C.dbg_taps:
            @blk.gpsimd
            def _(g):
                g.wait_ge(kb.sem["sV"], kb.c["sV"])
                g.wait_ge(kb.sem["sS"], kb.c["sS"])
                kb.inc(g.dma_start(out=io["dbg_xnfm"], in_=xn_fm[:, :]), "sg", 16)
                kb.inc(g.dma_start(out=io["dbg_wT8"], in_=wT8[0:8, :]), "sg", 16)
                kb.inc(g.dma_start(out=io["dbg_lns"], in_=lns[:, :]), "sg", 16)
                kb.inc(g.dma_start(out=io["dbg_xntm"], in_=xntm[:, :]), "sg", 16)
                kb.inc(g.dma_start(out=io["dbg_xin"], in_=xin[:, :]), "sg", 16)
                g.wait_ge(kb.sem["sg"], kb.c["sg"])


def _p1_proj(nc, kb, C, grp, io, sb):
    b = dict(kb.c)
    NQ = C.PHG * C.NQB
    NV = C.NKB * 4
    NKp = C.NKB * C.PHG
    VW = C.HG * 65
    xn_fm, xn_oth = sb["xn_fm"], sb["xn_oth"]
    K_sb, V_sb, Q_sb = sb["K"], sb["V"], sb["Q"]

    def xsrc(dc, t0, nn):
        if t0 < C.T:
            return xn_fm[:, dc * C.T + t0:dc * C.T + t0 + nn]
        return xn_oth[:, dc * C.T + t0 - C.T:dc * C.T + t0 - C.T + nn]

    rec = {"qps": [None] * NQ, "kps": [None] * NKp, "vps": [None] * NV}

    def s_ev(i):
        return b["sS"] + i + 1

    with contextlib.ExitStack() as ps:
        pa = [_psum(nc, ps, f"pa{grp}_{i}", 128, max(C.QB, C.KBLK), F32) for i in range(2)]
        pv = [_psum(nc, ps, f"pvv{grp}_{i}", 128, C.FH, F32) for i in range(2)]
        blk = ps.enter_context(nc.Block(f"p1p{grp}"))

        @blk.gpsimd
        def _(g):
            for wn, wsb in (("wq", "wq_g"), ("wk", "wk_g"), ("wv", "wv_g")):
                src = io[wn][:, grp * C.FH:(grp + 1) * C.FH].rearrange("(c p) f -> p c f", p=128)
                kb.inc(g.dma_start(
                    out=bass.AP(sb[wsb], 0, [[C.DC * C.FH, 128], [C.FH, C.DC], [1, C.FH]]),
                    in_=src), "sg", 16)
            if grp == 0:
                kb.inc(g.dma_start(out=sb["bvbc"][:, :], in_=io["bvbc"]), "sg", 16)

        @blk.tensor
        def _(t):
            kb.w(t, "sg", b["sg"] + 16)
            a_idx = 0
            for pp in range(C.PHG):
                for qb in range(C.NQB):
                    sl = pa[a_idx % 2]
                    if a_idx >= 2:
                        kb.w(t, "sS", s_ev(a_idx - 2))
                    for dc in range(C.DC):
                        ins = t.matmul(sl[:, 0:C.QB],
                                       sb["wq_g"][:, dc * C.FH + pp * 128:dc * C.FH + (pp + 1) * 128],
                                       xn_fm[:, dc * C.T + qb * C.QB:dc * C.T + (qb + 1) * C.QB],
                                       start=(dc == 0), stop=(dc == C.DC - 1))
                    rec["qps"][a_idx] = kb.inc(ins, "sT")
                    a_idx += 1
            kb.w(t, "sg", b["sg"] + 48)
            k_idx = 0
            v_idx = 0
            for j in range(C.NKB):
                for pp in range(C.PHG):
                    sl = pa[a_idx % 2]
                    if a_idx >= 2:
                        kb.w(t, "sS", s_ev(a_idx - 2))
                    for dc in range(C.DC):
                        ins = t.matmul(sl[:, 0:C.KBLK],
                                       sb["wk_g"][:, dc * C.FH + pp * 128:dc * C.FH + (pp + 1) * 128],
                                       xsrc(dc, j * C.KBLK, C.KBLK),
                                       start=(dc == 0), stop=(dc == C.DC - 1))
                    rec["kps"][k_idx] = kb.inc(ins, "sT")
                    a_idx += 1
                    k_idx += 1
                for tt in range(4):
                    sl = pv[v_idx % 2]
                    if v_idx >= 2:
                        kb.w(t, "sV", b["sV"] + (v_idx - 2) + 1)
                    for dc in range(C.DC):
                        ins = t.matmul(sl[:, 0:C.FH],
                                       xsrc(dc, j * C.KBLK + tt * 128, 128),
                                       sb["wv_g"][:, dc * C.FH:(dc + 1) * C.FH],
                                       start=(dc == 0), stop=(dc == C.DC - 1))
                    rec["vps"][v_idx] = kb.inc(ins, "sT")
                    v_idx += 1

        @blk.scalar
        def _(s):
            for i in range(NQ):
                pp, qb = divmod(i, C.NQB)
                pg = grp * C.PHG + pp
                kb.w(s, "sT", rec["qps"][i])
                ins = s.activation(
                    out=Q_sb[:, pp * C.T + qb * C.QB:pp * C.T + (qb + 1) * C.QB],
                    in_=pa[i % 2][:, 0:C.QB], func=AF.Identity,
                    bias=sb["bqt"][:, pg:pg + 1])
                kb.inc(ins, "sS")
            for kk in range(NKp):
                bi, pp = divmod(kk, C.PHG)
                pg = grp * C.PHG + pp
                kb.w(s, "sT", rec["kps"][kk])
                ins = s.activation(
                    out=K_sb[:, pp * C.S + bi * C.KBLK:pp * C.S + (bi + 1) * C.KBLK],
                    in_=pa[(NQ + kk) % 2][:, 0:C.KBLK], func=AF.Identity,
                    bias=sb["bkt"][:, pg:pg + 1])
                kb.inc(ins, "sS")

        @blk.vector
        def _(v):
            if grp == 0:
                v.memset(bass.AP(V_sb, 64, [[C.NKC * VW, 128], [65, C.NKC * C.HG], [1, 1]]), 1.0)
                v.memset(sb["rZ2"][:, :], 0.0)
            for v_idx in range(NV):
                kb.w(v, "sT", rec["vps"][v_idx])
                if grp == 0:
                    kb.w(v, "sg", b["sg"] + 64)
                kt = v_idx
                ins = v.scalar_tensor_tensor(
                    out=bass.AP(V_sb, kt * VW, [[C.NKC * VW, 128], [65, C.HG], [1, 64]]),
                    in0=pv[v_idx % 2][:, 0:C.FH], scalar=1.0,
                    in1=sb["bvbc"][:, grp * C.FH:(grp + 1) * C.FH],
                    op0=ALU.mult, op1=ALU.add)
                kb.inc(ins, "sV")

    if C.dbg and 'p1p' in C.dbg_taps:
        with nc.Block(f"p1pd{grp}") as blk:
            @blk.gpsimd
            def _(g):
                kb.inc(g.dma_start(out=io["dbg_K"][grp], in_=K_sb[:, :]), "sg", 16)
                kb.inc(g.dma_start(out=io["dbg_V"][grp], in_=V_sb[:, :]), "sg", 16)
                kb.inc(g.dma_start(out=io["dbg_Q"][grp], in_=Q_sb[:, :]), "sg", 16)
                g.wait_ge(kb.sem["sg"], kb.c["sg"])

def _p1_attn(nc, kb, C, grp, io, sb):
    b = dict(kb.c)
    VW = C.HG * 65
    K_sb, V_sb, Q_sb = sb["K"], sb["V"], sb["Q"]
    p_sb, rZ2, ctx = sb["p_sb"], sb["rZ2"], sb["ctx"]
    NPQ = C.PHG * C.NQB
    NGG = NPQ * 2 * C.NG
    rec = {"s": [None] * NGG, "av": [None] * NGG, "bc": [None] * NPQ}

    with contextlib.ExitStack() as ps:
        ps_s = [_psum(nc, ps, f"pss{grp}_{i}", 128, C.G * C.QB, F32) for i in range(2)]
        ps_ctx = [_psum(nc, ps, f"psc{grp}_{i}", 128, C.QB, F32) for i in range(2)]
        ps_bc = _psum(nc, ps, f"psbc{grp}", 128, C.QB, F32)
        blk = ps.enter_context(nc.Block(f"p1a{grp}"))

        its = []
        for p in range(C.PHG):
            for qb in range(C.NQB):
                for kc in range(C.NKC):
                    its.append((p, qb, p * C.NQB + qb, kc))
        NIT = len(its)
        rec["s"] = [None] * NIT
        rec["av"] = [None] * NIT

        @blk.tensor
        def _(t):
            def emit_S2(it):
                p, qb, i_pq, kc = its[it]
                sl = it % 2
                for hl in range(2):
                    ins = t.matmul(
                        ps_s[sl][:, hl * C.QB:(hl + 1) * C.QB],
                        K_sb[hl * 64:hl * 64 + 64,
                             p * C.S + kc * 128:p * C.S + (kc + 1) * 128],
                        Q_sb[hl * 64:hl * 64 + 64,
                             p * C.T + qb * C.QB:p * C.T + (qb + 1) * C.QB],
                        start=True, stop=True)
                rec["s"][it] = kb.inc(ins, "sT")

            def emit_AV2(it):
                p, qb, i_pq, kc = its[it]
                sl = it % 2
                kb.w(t, "sS", b["sS"] + it + 1)
                if kc == 0:
                    kb.w(t, "sV", b["sV"] + 2 * i_pq)
                for hl in range(2):
                    lh = p * 2 + hl
                    ins = t.matmul(
                        ps_ctx[hl][0:65, :],
                        V_sb[:, kc * VW + lh * 65:kc * VW + (lh + 1) * 65],
                        p_sb[:, ((it % 3) * 2 + hl) * C.QB:((it % 3) * 2 + hl + 1) * C.QB],
                        start=(kc == 0), stop=(kc == C.NKC - 1))
                rec["av"][it] = kb.inc(ins, "sT")

            emit_S2(0)
            for it in range(NIT):
                if it + 1 < NIT:
                    emit_S2(it + 1)
                emit_AV2(it)
                p, qb, i_pq, kc = its[it]
                if kc == C.NKC - 1:
                    kb.w(t, "sV", b["sV"] + 2 * i_pq + 1)
                    ins = t.matmul(ps_bc[:, 0:C.QB], sb["ebc"][0:33, :],
                                   rZ2[0:33, 0:C.QB], start=True, stop=True)
                    rec["bc"][i_pq] = kb.inc(ins, "sT")

        @blk.scalar
        def _(s):
            for it in range(NIT):
                sl = it % 2
                kb.w(s, "sT", rec["s"][it])
                ins = s.activation(out=p_sb[:, (it % 3) * 2 * C.QB:(it % 3 + 1) * 2 * C.QB],
                                   in_=ps_s[sl][:, 0:2 * C.QB], func=AF.Exp, scale=0.125)
                kb.inc(ins, "sS")

        @blk.vector
        def _(v):
            for p in range(C.PHG):
                for qb in range(C.NQB):
                    i_pq = p * C.NQB + qb
                    GG_last = (i_pq + 1) * C.NKC - 1
                    kb.w(v, "sT", rec["av"][GG_last])
                    with nc.allow_low_precision(reason="1/Z feeds a bf16 broadcast matmul"):
                        v.reciprocal(rZ2[0:1, 0:C.QB], ps_ctx[0][64:65, 0:C.QB])
                        ins = v.reciprocal(rZ2[32:33, 0:C.QB], ps_ctx[1][64:65, 0:C.QB])
                    kb.inc(ins, "sV")
                    kb.w(v, "sT", rec["bc"][i_pq])
                    pg = grp * C.PHG + p
                    with nc.allow_low_precision(reason="1/Z bcast is a bf16 scale factor"):
                        ins = v.tensor_copy(out=sb["bc_sb"][:, 0:C.QB], in_=ps_bc[:, 0:C.QB])
                    kb.vsync(v, ins)
                    v.tensor_mul(ctx[0:64, pg * C.T + qb * C.QB:pg * C.T + (qb + 1) * C.QB],
                                 ps_ctx[0][0:64, 0:C.QB], sb["bc_sb"][0:64, 0:C.QB])
                    ins = v.tensor_mul(
                        ctx[64:128, pg * C.T + qb * C.QB:pg * C.T + (qb + 1) * C.QB],
                        ps_ctx[1][0:64, 0:C.QB], sb["bc_sb"][64:128, 0:C.QB])
                    kb.inc(ins, "sV")


def _p2(nc, kb, C, io, sb):
    b = dict(kb.c)
    NT4 = C.QB // 128
    NTI = C.NQB * NT4

    def v_z(zi):
        qb, oc = divmod(zi, C.DC)
        return b["sV"] + qb * (C.DC + NT4 * 3) + oc + 1

    def v_zall(qb):
        return b["sV"] + qb * (C.DC + NT4 * 3) + C.DC

    def v_ztm(ti):
        qb, tt = divmod(ti, NT4)
        return b["sV"] + qb * (C.DC + NT4 * 3) + C.DC + tt * 3 + 1

    def v_rinv(ti):
        return v_ztm(ti) + 1

    def v_ast(ti):
        return v_ztm(ti) + 2

    rec = {"o": [None] * (C.NQB * C.DC), "tp": [None] * NTI}
    xn_fm, lns, ctx = sb["xn_fm"], sb["lns"], sb["ctx"]

    with contextlib.ExitStack() as ps:
        E2 = ps.enter_context
        wo_sb = E2(nc.sbuf_tensor("wo_sb", [128, C.DC * C.D], BF16))
        gb_sb = E2(nc.sbuf_tensor("gb_sb", [128, 2 * C.D], F32))
        z_fm = E2(nc.sbuf_tensor("z_fm", [128, 2 * C.DC * C.QB], F32))
        z_tm = E2(nc.sbuf_tensor("z_tm", [128, 2 * C.D], F32))
        t1 = E2(nc.sbuf_tensor("t1", [128, C.D], F32))
        sqd2 = E2(nc.sbuf_tensor("sqd2", [128, C.D], BF16))
        ast = E2(nc.sbuf_tensor("ast", [128, 2 * C.D], BF16))
        po = [_psum(nc, ps, f"po{i}", 128, C.QB, F32) for i in range(2)]
        pz = [_psum(nc, ps, f"pz{i}", 128, C.D, F32) for i in range(2)]
        blk = E2(nc.Block("p2"))

        @blk.gpsimd
        def _(g):
            kb.inc(g.dma_start(
                out=bass.AP(wo_sb, 0, [[C.DC * C.D, 128], [C.D, C.DC], [1, C.D]]),
                in_=io["wo"].rearrange("(c p) f -> p c f", p=128)), "sg", 16)

        @blk.sync
        def _(s):
            kb.inc(s.dma_start(out=gb_sb[:, :], in_=io["gbbc"]), "sx", 16)
            for ti in range(NTI):
                kb.w(s, "sV", v_ast(ti))
                kb.inc(s.dma_start(out=io["attn_scr"][ti * 128:(ti + 1) * 128, :],
                                   in_=ast[:, (ti % 2) * C.D:(ti % 2 + 1) * C.D]), "sx", 16)

        @blk.tensor
        def _(t):
            kb.w(t, "sg", b["sg"] + 16)
            for qb in range(C.NQB):
                for oc in range(C.DC):
                    zi = qb * C.DC + oc
                    if zi >= 2:
                        kb.w(t, "sV", v_z(zi - 2))
                    for cc in range(C.DC):
                        ins = t.matmul(po[zi % 2][:, 0:C.QB],
                                       wo_sb[:, cc * C.D + oc * 128:cc * C.D + (oc + 1) * 128],
                                       ctx[:, cc * C.T + qb * C.QB:cc * C.T + (qb + 1) * C.QB],
                                       start=(cc == 0), stop=(cc == C.DC - 1))
                    rec["o"][zi] = kb.inc(ins, "sT")
                for tt in range(NT4):
                    ti = qb * NT4 + tt
                    kb.w(t, "sV", v_zall(qb))
                    if ti >= 2:
                        kb.w(t, "sV", v_ztm(ti - 2))
                    for oc in range(C.DC):
                        base = (qb % 2) * C.DC * C.QB + oc * C.QB + tt * 128
                        ins = t.transpose(pz[ti % 2][:, oc * 128:(oc + 1) * 128],
                                          z_fm[:, base:base + 128], sb["idf"][:, :])
                    rec["tp"][ti] = kb.inc(ins, "sT")

        @blk.scalar
        def _(s):
            for ti in range(NTI):
                sl = ti % 2
                kb.w(s, "sV", v_ztm(ti))
                ins = s.activation(out=sqd2[:, :], in_=z_tm[:, sl * C.D:(sl + 1) * C.D],
                                   func=AF.Square,
                                   accum_out=lns[:, LNS_SSQ0 + sl:LNS_SSQ0 + sl + 1])
                kb.inc(ins, "sS")
                kb.w(s, "sV", v_rinv(ti))
                ins = s.activation(out=lns[:, LNS_R0:LNS_R0 + 1],
                                   in_=lns[:, LNS_RINV:LNS_RINV + 1], func=AF.Sqrt)
                kb.inc(ins, "sS")

        @blk.vector
        def _(v):
            kb.w(v, "sx", b["sx"] + 16)
            for qb in range(C.NQB):
                for oc in range(C.DC):
                    zi = qb * C.DC + oc
                    kb.w(v, "sT", rec["o"][zi])
                    if qb >= 2:
                        kb.w(v, "sT", rec["tp"][(qb - 2) * NT4 + NT4 - 1])
                    base = (qb % 2) * C.DC * C.QB + oc * C.QB
                    ins = v.scalar_tensor_tensor(
                        out=z_fm[:, base:base + C.QB],
                        in0=po[zi % 2][:, 0:C.QB], scalar=sb["bot"][:, oc:oc + 1],
                        in1=xn_fm[:, oc * C.T + qb * C.QB:oc * C.T + (qb + 1) * C.QB],
                        op0=ALU.add, op1=ALU.add)
                    kb.inc(ins, "sV")
                for tt in range(NT4):
                    ti = qb * NT4 + tt
                    sl = ti % 2
                    kb.w(v, "sT", rec["tp"][ti])
                    ins = v.tensor_copy(out=z_tm[:, sl * C.D:(sl + 1) * C.D],
                                        in_=pz[sl][:, 0:C.D])
                    n = kb.inc(ins, "sV")
                    kb.w(v, "sV", n)
                    kb.w(v, "sS", b["sS"] + 2 * ti + 1)
                    ins = _ln_stats(v, kb, lns, LNS_SSQ0 + sl,
                                    z_tm[:, sl * C.D:(sl + 1) * C.D], C.D)
                    kb.inc(ins, "sV")
                    kb.w(v, "sS", b["sS"] + 2 * ti + 2)
                    _ln_newton(v, kb, lns)
                    ins = v.scalar_tensor_tensor(out=t1[:, :], in0=z_tm[:, sl * C.D:(sl + 1) * C.D],
                                                 scalar=lns[:, LNS_MU:LNS_MU + 1],
                                                 in1=gb_sb[:, 0:C.D],
                                                 op0=ALU.subtract, op1=ALU.mult)
                    kb.vsync(v, ins)
                    if ti >= 2:
                        kb.w(v, "sx", b["sx"] + 16 * (1 + (ti - 2) + 1))
                    ins = v.scalar_tensor_tensor(out=ast[:, sl * C.D:(sl + 1) * C.D],
                                                 in0=t1[:, :],
                                                 scalar=lns[:, LNS_RSTD:LNS_RSTD + 1],
                                                 in1=gb_sb[:, C.D:2 * C.D],
                                                 op0=ALU.mult, op1=ALU.add)
                    kb.inc(ins, "sV")


def _p3(nc, kb, C, io, sb, p3st):
    b = dict(kb.c)
    PER_T = 1 + 2 * C.HC + C.DC
    PER_V = 1 + 2 * C.HC + C.DC

    def t_bc(esl):
        return b["sT"] + esl * PER_T + 1

    def t_u(esl, hc):
        return b["sT"] + esl * PER_T + 1 + 2 * hc + 1

    def t_v(esl, hc):
        return b["sT"] + esl * PER_T + 1 + 2 * hc + 2

    def t_y(esl, oc):
        return b["sT"] + esl * PER_T + 1 + 2 * C.HC + oc + 1

    def v_wbc(esl):
        return b["sV"] + esl * PER_V + 1

    def v_tt1(esl, hc):
        return b["sV"] + esl * PER_V + 1 + 2 * hc + 1

    def v_tt2(esl, hc):
        return b["sV"] + esl * PER_V + 1 + 2 * hc + 2

    def v_acc(esl, oc):
        return b["sV"] + esl * PER_V + 1 + 2 * C.HC + oc + 1

    def s_silu(esl, hc):
        return b["sS"] + esl * C.HC + hc + 1

    xn_fm, wT8, acc = sb["xn_fm"], sb["wT8"], sb["acc"]
    rg = {"w12": {}, "w1s": {}, "W3": {}}

    with contextlib.ExitStack() as ps:
        E3 = ps.enter_context
        W3_sb = E3(nc.sbuf_tensor("W3_sb", [128, 3 * C.HC * 128], BF16))
        h_sb = E3(nc.sbuf_tensor("h_sb", [128, C.HC * C.SL], BF16))
        su_sb = E3(nc.sbuf_tensor("su_sb", [128, 2 * C.SL], BF16))
        tb_sb = E3(nc.sbuf_tensor("tb_sb", [128, C.SL], BF16))
        wbc_sb = E3(nc.sbuf_tensor("wbc_sb", [128, C.SL], BF16))
        w1s = E3(nc.sbuf_tensor("w1s", [1, 2 * C.SL], BF16))
        pu = [_psum(nc, ps, f"pu{i}", 128, C.SL, F32) for i in range(2)]
        pvv = [_psum(nc, ps, f"pv3{i}", 128, C.SL, F32) for i in range(2)]
        py = [_psum(nc, ps, f"py{i}", 128, C.SL, F32) for i in range(3)]
        pb = _psum(nc, ps, "pb", 128, C.SL, F32)
        blk = E3(nc.Block("p3"))

        NW3 = 3
        H2 = C.HE // 2

        @blk.gpsimd
        def _(g):
            def load_w1s(esl):
                e, sl = divmod(esl, C.NSL)
                if esl >= 2:
                    kb.w(g, "sT", t_bc(esl - 2))
                n = kb.inc(g.dma_start(out=w1s[0:1, (esl % 2) * C.SL:(esl % 2 + 1) * C.SL],
                                       in_=wT8[e:e + 1, sl * C.SL:(sl + 1) * C.SL]),
                           "sg", 16)
                rg["w1s"][esl] = n

            def load_W3(esl, oc):
                e = esl // C.NSL
                w = esl * C.DC + oc
                if w >= NW3:
                    esl2, oc2 = divmod(w - NW3, C.DC)
                    kb.w(g, "sT", t_y(esl2, oc2))
                n = kb.inc(g.dma_start(
                    out=bass.AP(W3_sb, (w % NW3) * C.HC * 128,
                                [[NW3 * C.HC * 128, 128], [128, C.HC], [1, 128]]),
                    in_=bass.AP(io["W3"].tensor, e * C.HE * C.D + oc * 128,
                                [[C.D, 128], [128 * C.D, C.HC], [1, 128]])), "sg", 16)
                rg["W3"][w] = n

            for e in range(C.NE):
                load_w1s(e * C.NSL)
                for oc in range(min(NW3, C.DC)):
                    load_W3(e * C.NSL, oc)
                for half in range(2):
                    if e > 0:
                        kb.w(g, "sT", t_v((e - 1) * C.NSL + C.NSL - 1,
                                          (half + 1) * (C.HC // 2) - 1))
                    for wn, wsb in (("W1", "W1s"), ("W2", "W2s")):
                        n = kb.inc(g.dma_start(
                            out=bass.AP(sb[wsb], half * H2,
                                        [[C.DC * C.HE, 128], [C.HE, C.DC], [1, H2]]),
                            in_=io[wn][e][:, half * H2:(half + 1) * H2
                                          ].rearrange("(c p) h -> p c h", p=128)),
                            "sg", 16)
                    rg["w12"][(e, half)] = n
                for oc in range(min(NW3, C.DC), C.DC):
                    load_W3(e * C.NSL, oc)
                for sl in range(1, C.NSL):
                    esl = e * C.NSL + sl
                    load_w1s(esl)
                    for oc in range(C.DC):
                        load_W3(esl, oc)

        @blk.tensor
        def _(t):
            for e in range(C.NE):
                for sl in range(C.NSL):
                    esl = e * C.NSL + sl
                    kb.w(t, "sg", rg["w1s"][esl])
                    if esl >= 1:
                        kb.w(t, "sV", v_wbc(esl - 1))
                    ins = t.matmul(pb[:, 0:C.SL], sb["ones"][:, :],
                                   w1s[0:1, (esl % 2) * C.SL:(esl % 2 + 1) * C.SL],
                                   start=True, stop=True)
                    kb.inc(ins, "sT")
                    for hc in range(C.HC):
                        if sl == 0 and hc % (C.HC // 2) == 0:
                            kb.w(t, "sg", rg["w12"][(e, hc // (C.HC // 2))])
                        if hc >= 2:
                            kb.w(t, "sS", s_silu(esl, hc - 2))
                        elif esl >= 1:
                            kb.w(t, "sS", s_silu(esl - 1, C.HC - 2 + hc))
                        for dc in range(C.DC):
                            ins = t.matmul(pu[hc % 2][:, 0:C.SL],
                                           sb["W1s"][:, dc * C.HE + hc * 128:dc * C.HE + (hc + 1) * 128],
                                           xn_fm[:, dc * C.T + sl * C.SL:dc * C.T + (sl + 1) * C.SL],
                                           start=(dc == 0), stop=(dc == C.DC - 1))
                        kb.inc(ins, "sT")
                        if hc >= 2:
                            kb.w(t, "sV", v_tt1(esl, hc - 2))
                        elif esl >= 1:
                            kb.w(t, "sV", v_tt1(esl - 1, C.HC - 2 + hc))
                        for dc in range(C.DC):
                            ins = t.matmul(pvv[hc % 2][:, 0:C.SL],
                                           sb["W2s"][:, dc * C.HE + hc * 128:dc * C.HE + (hc + 1) * 128],
                                           xn_fm[:, dc * C.T + sl * C.SL:dc * C.T + (sl + 1) * C.SL],
                                           start=(dc == 0), stop=(dc == C.DC - 1))
                        kb.inc(ins, "sT")
                    kb.w(t, "sV", v_tt2(esl, C.HC - 1))
                    for oc in range(C.DC):
                        w = esl * C.DC + oc
                        kb.w(t, "sg", rg["W3"][w])
                        if w >= 3:
                            kb.w(t, "sV", v_acc(*divmod(w - 3, C.DC)))
                        for hc in range(C.HC):
                            ins = t.matmul(py[w % 3][:, 0:C.SL],
                                           bass.AP(W3_sb, (w % NW3) * C.HC * 128 + hc * 128,
                                                   [[NW3 * C.HC * 128, 128], [1, 128]]),
                                           h_sb[:, hc * C.SL:(hc + 1) * C.SL],
                                           start=(hc == 0), stop=(hc == C.HC - 1))
                        kb.inc(ins, "sT")
                    if esl == C.NE * C.NSL - 1:
                        ins = t.matmul(pb[:, 0:C.SL], sb["ones"][:, :],
                                       w1s[0:1, (esl % 2) * C.SL:(esl % 2 + 1) * C.SL],
                                       start=True, stop=True)
                        kb.inc(ins, "sT")

        @blk.scalar
        def _(s):
            for esl in range(C.NE * C.NSL):
                for hc in range(C.HC):
                    kb.w(s, "sT", t_u(esl, hc))
                    if hc >= 2:
                        kb.w(s, "sV", v_tt1(esl, hc - 2))
                    elif esl >= 1:
                        kb.w(s, "sV", v_tt1(esl - 1, C.HC - 2 + hc))
                    ins = s.activation(out=su_sb[:, (hc % 2) * C.SL:(hc % 2 + 1) * C.SL],
                                       in_=pu[hc % 2][:, 0:C.SL], func=AF.Silu)
                    kb.inc(ins, "sS")

        @blk.vector
        def _(v):
            for e in range(C.NE):
                for sl in range(C.NSL):
                    esl = e * C.NSL + sl
                    kb.w(v, "sT", t_bc(esl))
                    ins = v.tensor_copy(out=wbc_sb[:, :], in_=pb[:, 0:C.SL])
                    n = kb.inc(ins, "sV")
                    kb.w(v, "sV", n)
                    for hc in range(C.HC):
                        kb.w(v, "sT", t_v(esl, hc))
                        kb.w(v, "sS", s_silu(esl, hc))
                        ins = v.tensor_mul(tb_sb[:, :], pvv[hc % 2][:, 0:C.SL],
                                           su_sb[:, (hc % 2) * C.SL:(hc % 2 + 1) * C.SL])
                        n = kb.inc(ins, "sV")
                        kb.w(v, "sV", n)
                        if hc == 0 and esl >= 1:
                            kb.w(v, "sT", t_y(esl - 1, C.DC - 1))
                        ins = v.tensor_mul(h_sb[:, hc * C.SL:(hc + 1) * C.SL],
                                           tb_sb[:, :], wbc_sb[:, :])
                        if hc == C.HC - 1:
                            kb.vsync(v, ins)
                            v.sem_inc(kb.sem["sV"], 1)
                            kb.c["sV"] += 1
                        else:
                            kb.inc(ins, "sV")
                    for oc in range(C.DC):
                        if oc < C.DC - 1:
                            kb.w(v, "sT", t_y(esl, oc + 1))
                        elif esl < C.NE * C.NSL - 1:
                            kb.w(v, "sT", t_bc(esl + 1))
                        else:
                            kb.w(v, "sT", t_y(esl, oc) + 1)
                        dst = acc[:, oc * C.T + sl * C.SL:oc * C.T + (sl + 1) * C.SL]
                        w = esl * C.DC + oc
                        if e == 0:
                            ins = v.tensor_copy(out=dst, in_=py[w % 3][:, 0:C.SL])
                        else:
                            ins = v.tensor_add(dst, dst, py[w % 3][:, 0:C.SL])
                        kb.inc(ins, "sV")


def _p4(nc, kb, C, io, sb):
    b = dict(kb.c)
    NT = C.NTT
    rec = {"tp": [None] * NT}
    acc = sb["acc"]

    with contextlib.ExitStack() as ps:
        E4 = ps.enter_context
        xin4 = E4(nc.sbuf_tensor("xin4", [128, 2 * C.D], F32))
        atn4 = E4(nc.sbuf_tensor("atn4", [128, 2 * C.D], BF16))
        tmp4 = E4(nc.sbuf_tensor("tmp4", [128, C.D], F32))
        out4 = E4(nc.sbuf_tensor("out4", [128, 2 * C.D], F32))
        pz4 = [_psum(nc, ps, f"pz4{i}", 128, C.D, F32) for i in range(2)]
        blk = E4(nc.Block("p4"))

        @blk.sync
        def _(s):
            for i in range(NT + 1):
                if i < NT:
                    sl = i % 2
                    if i >= 2:
                        kb.w(s, "sV", b["sV"] + (i - 2) + 1)
                    kb.inc(s.dma_start(out=xin4[:, sl * C.D:(sl + 1) * C.D],
                                       in_=io["x_seq"][i * 128:(i + 1) * 128, :]), "sx", 16)
                    kb.inc(s.dma_start(out=atn4[:, sl * C.D:(sl + 1) * C.D],
                                       in_=io["attn_scr"][i * 128:(i + 1) * 128, :]), "sx", 16)
                if i >= 1:
                    j = i - 1
                    kb.w(s, "sV", b["sV"] + j + 1)
                    kb.inc(s.dma_start(out=io["out"][j * 128:(j + 1) * 128, :],
                                       in_=out4[:, (j % 2) * C.D:(j % 2 + 1) * C.D]), "sx", 16)
            s.wait_ge(kb.sem["sx"], kb.c["sx"])

        @blk.tensor
        def _(t):
            for i in range(NT):
                sl = i % 2
                if i >= 2:
                    kb.w(t, "sV", b["sV"] + (i - 2) + 1)
                for oc in range(C.DC):
                    ins = t.transpose(pz4[sl][:, oc * 128:(oc + 1) * 128],
                                      acc[:, oc * C.T + i * 128:oc * C.T + (i + 1) * 128],
                                      sb["idf"][:, :])
                rec["tp"][i] = kb.inc(ins, "sT")

        @blk.vector
        def _(v):
            for i in range(NT):
                sl = i % 2
                kb.w(v, "sT", rec["tp"][i])
                kb.w(v, "sx", b["sx"] + 16 * (2 if i == 0 else 3 * i + 1))
                if i >= 2:
                    kb.w(v, "sx", b["sx"] + 16 * (3 * (i - 2) + 5))
                ins = v.tensor_add(tmp4[:, :], pz4[sl][:, 0:C.D],
                                    xin4[:, sl * C.D:(sl + 1) * C.D])
                kb.vsync(v, ins)
                ins = v.tensor_add(out4[:, sl * C.D:(sl + 1) * C.D], tmp4[:, :],
                                   atn4[:, sl * C.D:(sl + 1) * C.D])
                kb.inc(ins, "sV")


def _aux_inputs(C: Cfg):
    bf = ml_dtypes.bfloat16
    ebc = np.zeros((33, 128), bf)
    ebc[0, 0:64] = 1
    ebc[32, 64:128] = 1
    return {
        "identbf": np.eye(128, dtype=bf),
        "identf32": np.eye(128, dtype=np.float32),
        "ones1": np.ones((1, 128), bf),
        "ebc": ebc,
    }


def _layout_inputs(C: Cfg, inp):
    f = lambda k: np.ascontiguousarray(np.asarray(inp[k], np.float32))
    com = {
        "wq": f("L0_attn_wq"), "wk": f("L0_attn_wk"), "wv": f("L0_attn_wv"),
        "wo": f("L0_attn_wo"), "Wg": f("L0_ffn_Wg"),
        "W1": f("L0_ffn_W1"), "W2": f("L0_ffn_W2"), "W3": f("L0_ffn_W3"),
        "bqt": np.ascontiguousarray(f("L0_attn_bq").reshape(C.DC, 128).T),
        "bkt": np.ascontiguousarray(f("L0_attn_bk").reshape(C.DC, 128).T),
        "bot": np.ascontiguousarray(f("L0_attn_bo").reshape(C.DC, 128).T),
        "gbbc": np.ascontiguousarray(np.concatenate(
            [np.broadcast_to(f("L0_attn_gamma")[None, :], (128, C.D)),
             np.broadcast_to(f("L0_attn_beta")[None, :], (128, C.D))], axis=1)),
        "bvbc": np.ascontiguousarray(np.broadcast_to(f("L0_attn_bv")[None, :], (128, C.D))),
    }
    com.update(_aux_inputs(C))
    return com


def kernel(**inputs):
    C = Cfg()
    nc = build_program(C)
    x = np.asarray(inputs["L0_attn_x"], np.float32)
    B = x.shape[0]
    com = _layout_inputs(C, inputs)
    in_maps = []
    for c in range(2 * B):
        bi, hp = divmod(c, 2)
        xs = x[bi]
        if hp == 1:
            xs = np.concatenate([xs[C.T:], xs[:C.T]], axis=0)
        m = dict(com)
        m["x_seq"] = np.ascontiguousarray(xs)
        in_maps.append(m)
    res = run_bass_kernel_spmd(nc, in_maps, core_ids=list(range(2 * B)))
    out = np.empty((B, C.S, C.D), np.float32)
    for c in range(2 * B):
        bi, hp = divmod(c, 2)
        out[bi, hp * C.T:(hp + 1) * C.T] = res.results[c]["out"]
    return out
